# version 01
# baseline (speedup 1.0000x reference)
import numpy as np
import ml_dtypes
import concourse.bass as bass
import concourse.mybir as mybir
from concourse.bass_utils import run_bass_kernel_spmd

F32 = mybir.dt.float32
BF16 = mybir.dt.bfloat16
I32 = mybir.dt.int32
AF = mybir.ActivationFunctionType
ALU = mybir.AluOpType
AX = mybir.AxisListType
BF = ml_dtypes.bfloat16

NCORES = 8
TOK = 1024
D = 2048
NF = D // 128
LN_EPS = 1e-5


class Pr:
    def __init__(self):
        self.nc = bass.Bass("TRN2", target_bir_lowering=False, debug=False)
        self.sems = {}
        self.cnt = {}

    def sem(self, name):
        if name not in self.sems:
            self.sems[name] = self.nc.alloc_semaphore(name)
            self.cnt[name] = 0
        return self.sems[name]

    def inc(self, name, inst, n=1):
        inst.then_inc(self.sem(name), n)
        self.cnt[name] += n
        return inst

    def wait(self, eng, name, val=None):
        eng.wait_ge(self.sem(name), self.cnt[name] if val is None else val)

    def val(self, name):
        self.sem(name)
        return self.cnt[name]

    def chain(self, eng, name, inst):
        self.inc(name, inst)
        self.wait(eng, name)


def emit_ln(pr, tag, xres, xln, ones_f, sq, row_m, row_v, ps_sum, ps_sq, ps_bc,
            ncols, waits, eps_ap):
    nc = pr.nc
    nch = (ncols + 511) // 512
    s_sq = tag + "sq"
    s_pe = tag + "pe"
    s_st = tag + "st"
    s_dv = tag + "dv"
    s_rt = tag + "rt"
    s_bc = tag + "bc"
    s_dn = tag + "dn"
    b_sq = pr.val(s_sq)
    b_pe = pr.val(s_pe)

    with nc.Block() as blk:
        @blk.scalar
        def _(eng):
            for s, v in waits:
                eng.wait_ge(pr.sem(s), v)
            for j in range(NF):
                half = j % 2
                if j >= 2:
                    eng.wait_ge(pr.sem(s_pe), b_pe + j - 1)
                pr.inc(s_sq, eng.activation(sq[:, half * ncols:(half + 1) * ncols],
                                            xres[:, j * ncols:(j + 1) * ncols], AF.Square))

        @blk.tensor
        def _(eng):
            for s, v in waits:
                eng.wait_ge(pr.sem(s), v)
            for j in range(NF):
                half = j % 2
                xt = xres[:, j * ncols:(j + 1) * ncols]
                for c in range(nch):
                    c0, c1 = c * 512, min((c + 1) * 512, ncols)
                    eng.matmul(ps_sum[0:1, c0:c1], ones_f[:, 0:1], xt[:, c0:c1],
                               start=(j == 0), stop=(j == NF - 1))
                eng.wait_ge(pr.sem(s_sq), b_sq + j + 1)
                sqt = sq[:, half * ncols:(half + 1) * ncols]
                inst = None
                for c in range(nch):
                    c0, c1 = c * 512, min((c + 1) * 512, ncols)
                    inst = eng.matmul(ps_sq[0:1, c0:c1], ones_f[:, 0:1], sqt[:, c0:c1],
                                      start=(j == 0), stop=(j == NF - 1))
                pr.inc(s_pe, inst)

    with nc.Block() as blk:
        @blk.vector
        def _(eng):
            eng.wait_ge(pr.sem(s_pe), b_pe + NF)
            pr.chain(eng, "vq", eng.tensor_scalar(row_m[0:1, 0:ncols], ps_sum[0:1, 0:ncols], 1.0 / D, None, ALU.mult))
            pr.chain(eng, "vq", eng.tensor_scalar(row_v[0:1, 0:ncols], ps_sq[0:1, 0:ncols], 1.0 / D, None, ALU.mult))
            pr.chain(eng, "vq", eng.tensor_mul(sq[0:1, 0:ncols], row_m[0:1, 0:ncols], row_m[0:1, 0:ncols]))
            pr.inc(s_dv, eng.tensor_sub(row_v[0:1, 0:ncols], row_v[0:1, 0:ncols],
                                        sq[0:1, 0:ncols]))

        @blk.scalar
        def _(eng):
            pr.wait(eng, s_dv)
            pr.inc(s_rt, eng.activation(row_v[0:1, 0:ncols], row_v[0:1, 0:ncols],
                                        AF.Sqrt, bias=eps_ap))

    with nc.Block() as blk:
        @blk.vector
        def _(eng):
            pr.wait(eng, s_rt)
            pr.inc(s_rt, eng.reciprocal(row_v[0:1, 0:ncols], row_v[0:1, 0:ncols]))

        @blk.tensor
        def _(eng):
            pr.wait(eng, s_rt)
            inst = None
            for c in range(nch):
                c0, c1 = c * 512, min((c + 1) * 512, ncols)
                eng.matmul(ps_bc[:, c0:c1], ones_f[0:1, :], row_m[0:1, c0:c1],
                           start=True, stop=True)
                inst = eng.matmul(ps_bc[:, ncols + c0:ncols + c1], ones_f[0:1, :],
                                  row_v[0:1, c0:c1], start=True, stop=True)
            pr.inc(s_bc, inst)

    with nc.Block() as blk:
        @blk.vector
        def _(eng):
            pr.wait(eng, s_bc)
            inst = None
            for j in range(NF):
                xt = xres[:, j * ncols:(j + 1) * ncols]
                ot = xln[:, j * ncols:(j + 1) * ncols]
                tmp = sq[:, (j % 2) * ncols:((j % 2) + 1) * ncols]
                pr.chain(eng, "vq", eng.tensor_sub(tmp, xt, ps_bc[:, 0:ncols]))
                inst = eng.tensor_mul(ot, tmp, ps_bc[:, ncols:2 * ncols])
                if j + 2 < NF:
                    pr.chain(eng, "vq", inst)
            pr.inc(s_dn, inst)


def build_l0ffn():
    pr = Pr()
    nc = pr.nc
    xf = nc.dram_tensor("xf", [D, TOK], F32, kind="ExternalInput").ap()
    wgT = nc.dram_tensor("wgT", [D, 8192], I32, kind="ExternalInput").ap()
    wuT = nc.dram_tensor("wuT", [D, 8192], I32, kind="ExternalInput").ap()
    wdT = nc.dram_tensor("wdT", [8192, D], I32, kind="ExternalInput").ap()
    gsc = nc.dram_tensor("gsc", [128, 64], F32, kind="ExternalInput").ap()
    usc = nc.dram_tensor("usc", [128, 64], F32, kind="ExternalInput").ap()
    dsc = nc.dram_tensor("dsc", [128, 16], F32, kind="ExternalInput").ap()
    onesf = nc.dram_tensor("onesf", [128, 128], F32, kind="ExternalInput").ap()
    cstf = nc.dram_tensor("cstf", [128, 8], F32, kind="ExternalInput").ap()
    xo = nc.dram_tensor("xo", [D, TOK], F32, kind="ExternalOutput").ap()

    xres = nc.alloc_sbuf_tensor("xres", [128, NF * TOK], F32)
    xln = nc.alloc_sbuf_tensor("xln", [128, NF * TOK], BF16)
    sq = nc.alloc_sbuf_tensor("sq", [128, 2 * TOK], F32)
    row_m = nc.alloc_sbuf_tensor("row_m", [1, TOK], F32)
    row_v = nc.alloc_sbuf_tensor("row_v", [1, TOK], F32)
    ones = nc.alloc_sbuf_tensor("ones", [128, 128], F32)
    cst = nc.alloc_sbuf_tensor("cst", [128, 8], F32)
    g_sc = nc.alloc_sbuf_tensor("g_sc", [128, 64], F32)
    u_sc = nc.alloc_sbuf_tensor("u_sc", [128, 64], F32)
    d_sc = nc.alloc_sbuf_tensor("d_sc", [128, 16], F32)
    FB = 1024
    NFB = 8192 // FB
    FT = FB // 128
    h = nc.alloc_sbuf_tensor("h", [128, FT * TOK], BF16)
    hg = nc.alloc_sbuf_tensor("hg", [128, TOK], BF16)
    wg_i = nc.alloc_sbuf_tensor("wg_i", [128, 2048], I32)
    wu_i = nc.alloc_sbuf_tensor("wu_i", [128, 2048], I32)
    wg_b = nc.alloc_sbuf_tensor("wg_b", [128, 2048], BF16)
    wu_b = nc.alloc_sbuf_tensor("wu_b", [128, 2048], BF16)
    wd_i = nc.alloc_sbuf_tensor("wd_i", [128, 2048], I32)
    wd_blk = nc.alloc_sbuf_tensor("wd_blk", [128, FT * 2048], BF16)

    pool = nc.alloc_psum_tensor("pool", [128, 4096], F32)
    ps_sum = pool[0:1, 0:TOK]
    ps_sq = pool[0:1, TOK:2 * TOK]
    ps_bc = pool[:, 2 * TOK:4 * TOK]
    ps_g = pool[:, 0:TOK]
    ps_u = pool[:, TOK:2 * TOK]

    with nc.Block() as blk:
        @blk.sync
        def _(eng):
            for j in range(NF):
                pr.inc("d_x", eng.dma_start(xres[:, j * TOK:(j + 1) * TOK],
                                            xf[j * 128:(j + 1) * 128, :]), 16)
            pr.inc("d_c", eng.dma_start(ones[:], onesf), 16)
            pr.inc("d_c", eng.dma_start(cst[:], cstf), 16)
            pr.inc("d_c", eng.dma_start(g_sc[:], gsc), 16)
            pr.inc("d_c", eng.dma_start(u_sc[:], usc), 16)
            pr.inc("d_c", eng.dma_start(d_sc[:], dsc), 16)

    emit_ln(pr, "ln0", xres[:], xln[:], ones[:], sq[:], row_m[:], row_v[:],
            ps_sum, ps_sq, ps_bc, TOK,
            [("d_x", pr.val("d_x")), ("d_c", pr.val("d_c"))], cst[0:1, 0:1])

    for fb in range(NFB):
        for j in range(FT):
            ft = fb * FT + j
            with nc.Block() as blk:
                @blk.sync
                def _(eng):
                    if ft > 0:
                        pr.wait(eng, "pe_gu")
                    pr.inc("d_wg", eng.dma_start(
                        wg_i[:].rearrange("p (kc f) -> p kc f", kc=16),
                        wgT[:, ft * 128:(ft + 1) * 128].rearrange("(kc kp) f -> kp kc f", kp=128)), 16)
                    pr.inc("d_wu", eng.dma_start(
                        wu_i[:].rearrange("p (kc f) -> p kc f", kc=16),
                        wuT[:, ft * 128:(ft + 1) * 128].rearrange("(kc kp) f -> kp kc f", kp=128)), 16)

                @blk.scalar
                def _(eng):
                    pr.wait(eng, "d_wg")
                    pr.wait(eng, "d_wu")
                    eng.activation(wg_b[:], wg_i[:], AF.Copy, bias=-128.0)
                    pr.inc("deq", eng.activation(wu_b[:], wu_i[:], AF.Copy, bias=-128.0))
                    eng.wait_ge(pr.sem("pe_g"), ft + 1)
                    eng.wait_ge(pr.sem("dve_gu"), ft)
                    pr.inc("act_g", eng.activation(hg[:], ps_g[:], AF.Silu,
                                                   scale=g_sc[:, ft:ft + 1]))

                @blk.tensor
                def _(eng):
                    pr.wait(eng, "deq")
                    if ft == 0:
                        pr.wait(eng, "ln0dn")
                    if j == 0 and fb > 0:
                        pr.wait(eng, "dve_d")
                    eng.wait_ge(pr.sem("act_g"), ft)
                    eng.wait_ge(pr.sem("dve_gu"), ft)
                    inst = None
                    for tc in range(2):
                        ts = slice(tc * 512, (tc + 1) * 512)
                        for kc in range(16):
                            inst = eng.matmul(ps_g[:, ts], wg_b[:, kc * 128:(kc + 1) * 128],
                                       xln[:, kc * TOK + tc * 512: kc * TOK + (tc + 1) * 512],
                                       start=(kc == 0), stop=(kc == 15))
                    pr.inc("pe_g", inst)
                    for tc in range(2):
                        ts = slice(tc * 512, (tc + 1) * 512)
                        for kc in range(16):
                            inst = eng.matmul(ps_u[:, ts], wu_b[:, kc * 128:(kc + 1) * 128],
                                       xln[:, kc * TOK + tc * 512: kc * TOK + (tc + 1) * 512],
                                       start=(kc == 0), stop=(kc == 15))
                    pr.inc("pe_gu", inst)

                @blk.vector
                def _(eng):
                    eng.wait_ge(pr.sem("pe_gu"), ft + 1)
                    eng.wait_ge(pr.sem("act_g"), ft + 1)
                    if j == 0:
                        pr.wait(eng, "pe_d")
                    pr.inc("dve_gu", eng.scalar_tensor_tensor(
                        h[:, j * TOK:(j + 1) * TOK], ps_u[:],
                        u_sc[:, ft:ft + 1], hg[:], ALU.mult, ALU.mult))

        with nc.Block() as blk:
            @blk.sync
            def _(eng):
                b_deq = pr.val("deq_d")
                for fc in range(FT):
                    gfc = fb * FT + fc
                    if fb > 0 and fc == 0:
                        pr.wait(eng, "pe_d")
                    if b_deq + fc > 0:
                        eng.wait_ge(pr.sem("deq_d"), b_deq + fc)
                    pr.inc("d_wd", eng.dma_start(wd_i[:], wdT[gfc * 128:(gfc + 1) * 128, :]), 16)

            @blk.scalar
            def _(eng):
                b_d = pr.val("d_wd") - 16 * FT
                for fc in range(FT):
                    eng.wait_ge(pr.sem("d_wd"), b_d + 16 * (fc + 1))
                    pr.inc("deq_d", eng.activation(wd_blk[:, fc * 2048:(fc + 1) * 2048],
                                                   wd_i[:], AF.Copy, bias=-128.0))

        b_dve = pr.val("dve_d")
        b_pe = pr.val("pe_d")
        for dg in range(4):
            with nc.Block() as blk:
                @blk.tensor
                def _(eng):
                    pr.wait(eng, "deq_d")
                    if dg == 0:
                        pr.wait(eng, "dve_gu")
                    if dg > 0:
                        eng.wait_ge(pr.sem("dve_d"), b_dve + 4 * dg)
                    inst = None
                    for dl in range(4):
                        dt = dg * 4 + dl
                        psd = pool[:, dl * TOK:(dl + 1) * TOK]
                        for fc in range(FT):
                            for tc in range(2):
                                ts = slice(tc * 512, (tc + 1) * 512)
                                inst = eng.matmul(psd[:, ts],
                                           wd_blk[:, fc * 2048 + dt * 128: fc * 2048 + (dt + 1) * 128],
                                           h[:, fc * TOK + tc * 512: fc * TOK + (tc + 1) * 512],
                                           start=(fc == 0), stop=(fc == FT - 1))
                    pr.inc("pe_d", inst)

                @blk.vector
                def _(eng):
                    eng.wait_ge(pr.sem("pe_d"), b_pe + dg + 1)
                    for dl in range(4):
                        dt = dg * 4 + dl
                        psd = pool[:, dl * TOK:(dl + 1) * TOK]
                        pr.inc("dve_d", eng.scalar_tensor_tensor(
                            xres[:, dt * TOK:(dt + 1) * TOK], psd,
                            d_sc[:, dt:dt + 1], xres[:, dt * TOK:(dt + 1) * TOK],
                            ALU.mult, ALU.add))

    with nc.Block() as blk:
        @blk.sync
        def _(eng):
            pr.wait(eng, "dve_d")
            for j in range(NF):
                pr.inc("d_out", eng.dma_start(xo[j * 128:(j + 1) * 128, :],
                                              xres[:, j * TOK:(j + 1) * TOK]), 16)
            pr.wait(eng, "d_out")
    return nc


SCALE_ATT = 0.08838834764831845


def build_l1attn():
    pr = Pr()
    nc = pr.nc
    xf = nc.dram_tensor("xf", [D, TOK], F32, kind="ExternalInput").ap()
    kTf = nc.dram_tensor("kTf", [16, 128, 4096], F32, kind="ExternalInput").ap()
    vTf = nc.dram_tensor("vTf", [16, 4096, 128], F32, kind="ExternalInput").ap()
    wqT = nc.dram_tensor("wqT", [D, D], F32, kind="ExternalInput").ap()
    woT = nc.dram_tensor("woT", [D, D], F32, kind="ExternalInput").ap()
    onesf = nc.dram_tensor("onesf", [128, 128], F32, kind="ExternalInput").ap()
    onesb = nc.dram_tensor("onesb", [128, 128], BF16, kind="ExternalInput").ap()
    cstf = nc.dram_tensor("cstf", [128, 8], F32, kind="ExternalInput").ap()
    xo = nc.dram_tensor("xo", [D, TOK], F32, kind="ExternalOutput").ap()

    xres = nc.alloc_sbuf_tensor("xres", [128, NF * TOK], F32)
    xab = nc.alloc_sbuf_tensor("xab", [128, NF * TOK], BF16)
    qf = nc.alloc_sbuf_tensor("qf", [128, NF * TOK], BF16)
    sq = nc.alloc_sbuf_tensor("sq", [128, 2 * TOK], F32)
    row_m = nc.alloc_sbuf_tensor("row_m", [1, TOK], F32)
    row_v = nc.alloc_sbuf_tensor("row_v", [1, TOK], F32)
    srow = nc.alloc_sbuf_tensor("srow", [1, TOK], F32)
    ones = nc.alloc_sbuf_tensor("ones", [128, 128], F32)
    onebv = nc.alloc_sbuf_tensor("onebv", [128, 128], BF16)
    cst = nc.alloc_sbuf_tensor("cst", [128, 8], F32)
    stg = nc.alloc_sbuf_tensor("stg", [128, 2048], F32)
    w_b = nc.alloc_sbuf_tensor("w_b", [128, 2048], BF16)
    kb = nc.alloc_sbuf_tensor("kb", [128, 4096], BF16)
    vb = nc.alloc_sbuf_tensor("vb", [128, 4096], BF16)
    ptq = nc.alloc_sbuf_tensor("ptq", [128, 8 * TOK], BF16)
    rrb = nc.alloc_sbuf_tensor("rrb", [128, TOK], F32)
    negb = nc.alloc_sbuf_tensor("negb", [128, 1], F32)

    pool = nc.alloc_psum_tensor("pool", [128, 4096], F32)
    ps_sum = pool[0:1, 0:TOK]
    ps_sq = pool[0:1, TOK:2 * TOK]
    ps_bc = pool[:, 2 * TOK:4 * TOK]
    ps_q = pool[:, 0:TOK]
    ps_o = pool[:, TOK:2 * TOK]
    ps_row = pool[0:1, 2 * TOK:3 * TOK]
    ps_n = pool[0:1, 3 * TOK:4 * TOK]
    ps_nb = pool[:, 3 * TOK:3 * TOK + 1]

    with nc.Block() as blk:
        @blk.sync
        def _(eng):
            for j in range(NF):
                pr.inc("d_x", eng.dma_start(xres[:, j * TOK:(j + 1) * TOK],
                                            xf[j * 128:(j + 1) * 128, :]), 16)
            pr.inc("d_c", eng.dma_start(ones[:], onesf), 16)
            pr.inc("d_c", eng.dma_start(onebv[:], onesb), 16)
            pr.inc("d_c", eng.dma_start(cst[:], cstf), 16)

    emit_ln(pr, "ln1", xres[:], xab[:], ones[:], sq[:], row_m[:], row_v[:],
            ps_sum, ps_sq, ps_bc, TOK,
            [("d_x", pr.val("d_x")), ("d_c", pr.val("d_c"))], cst[0:1, 0:1])

    for od in range(16):
        with nc.Block() as blk:
            @blk.sync
            def _(eng):
                if od > 0:
                    eng.wait_ge(pr.sem("v_wb"), od)
                pr.inc("d_wq", eng.dma_start(
                    stg[:].rearrange("p (kc f) -> p kc f", kc=16),
                    wqT[:, od * 128:(od + 1) * 128].rearrange("(kc kp) f -> kp kc f", kp=128)), 16)

            @blk.vector
            def _(eng):
                eng.wait_ge(pr.sem("d_wq"), 16 * (od + 1))
                if od > 0:
                    eng.wait_ge(pr.sem("pe_qm"), od)
                pr.inc("v_wb", eng.tensor_copy(w_b[:], stg[:]))
                eng.wait_ge(pr.sem("pe_qm"), od + 1)
                pr.inc("v_qf", eng.tensor_copy(qf[:, od * TOK:(od + 1) * TOK], ps_q[:]))

            @blk.tensor
            def _(eng):
                eng.wait_ge(pr.sem("v_wb"), od + 1)
                if od == 0:
                    pr.wait(eng, "ln1dn")
                eng.wait_ge(pr.sem("v_qf"), od)
                inst = None
                for tc in range(2):
                    ts = slice(tc * 512, (tc + 1) * 512)
                    for kc in range(16):
                        inst = eng.matmul(ps_q[:, ts], w_b[:, kc * 128:(kc + 1) * 128],
                                          xab[:, kc * TOK + tc * 512: kc * TOK + (tc + 1) * 512],
                                          start=(kc == 0), stop=(kc == 15))
                pr.inc("pe_qm", inst)

    for h in range(16):
        with nc.Block() as blk:
            @blk.sync
            def _(eng):
                if h == 0:
                    eng.wait_ge(pr.sem("v_wb"), 16)
                else:
                    eng.wait_ge(pr.sem("v_vc"), 2 * h)
                pr.inc("d_kv", eng.dma_start(stg[:], kTf[h, :, 0:2048]), 16)
                eng.wait_ge(pr.sem("pe_n"), 5 * h + 2)
                pr.inc("d_kv", eng.dma_start(stg[:], kTf[h, :, 2048:4096]), 16)
                eng.wait_ge(pr.sem("pe_n"), 5 * h + 4)
                pr.inc("d_kv", eng.dma_start(
                    stg[:].rearrange("p (c d) -> p c d", c=16),
                    vTf[h, 0:2048, :].rearrange("(c p) d -> p c d", p=128)), 16)
                eng.wait_ge(pr.sem("v_vc"), 2 * h + 1)
                pr.inc("d_kv", eng.dma_start(
                    stg[:].rearrange("p (c d) -> p c d", c=16),
                    vTf[h, 2048:4096, :].rearrange("(c p) d -> p c d", p=128)), 16)

            @blk.scalar
            def _(eng):
                if h == 0:
                    pr.wait(eng, "ln1dn")
                for half in range(2):
                    eng.wait_ge(pr.sem("v_kc"), 2 * h + half + 1)
                    pr.inc("s_kq", eng.activation(stg[:], kb[:, half * 2048:(half + 1) * 2048],
                                                  AF.Square))
                eng.wait_ge(pr.sem("pe_n"), 5 * h)
                eng.wait_ge(pr.sem("v_qf"), h + 1)
                pr.inc("s_kq", eng.activation(sq[:, 0:TOK], qf[:, h * TOK:(h + 1) * TOK],
                                              AF.Square))
                eng.wait_ge(pr.sem("v_n"), 7 * h + 6)
                pr.inc("s_rt2", eng.activation(row_m[0:1, 3:4], row_m[0:1, 2:3], AF.Sqrt))
                for qt in range(4):
                    for lc in range(8):
                        ktc = qt * 8 + lc
                        eng.wait_ge(pr.sem("pe_s"), 32 * h + ktc + 1)
                        if ktc == 0:
                            eng.wait_ge(pr.sem("v_bb"), h + 1)
                        if qt > 0:
                            eng.wait_ge(pr.sem("pe_pv"), 32 * h + (qt - 1) * 8 + lc + 1)
                        pr.inc("s_exp", eng.activation(
                            ptq[:, lc * TOK:(lc + 1) * TOK], ps_q[:], AF.Exp,
                            bias=negb[:, 0:1]))

            @blk.tensor
            def _(eng):
                for half in range(2):
                    eng.wait_ge(pr.sem("s_kq"), 3 * h + half + 1)
                    for c in range(2):
                        i = 2 * half + c
                        eng.wait_ge(pr.sem("v_n"), 7 * h + i)
                        inst = None
                        for sc in range(2):
                            cs = slice(c * 1024 + sc * 512, c * 1024 + (sc + 1) * 512)
                            os_ = slice(sc * 512, (sc + 1) * 512)
                            inst = eng.matmul(ps_n[0:1, os_], ones[:, 0:1], stg[:, cs],
                                              start=True, stop=True)
                        pr.inc("pe_n", inst)
                eng.wait_ge(pr.sem("s_kq"), 3 * h + 3)
                eng.wait_ge(pr.sem("v_n"), 7 * h + 4)
                inst = None
                for sc in range(2):
                    os_ = slice(sc * 512, (sc + 1) * 512)
                    inst = eng.matmul(ps_n[0:1, os_], ones[:, 0:1], sq[:, sc * 512:(sc + 1) * 512],
                                      start=True, stop=True)
                pr.inc("pe_n", inst)
                eng.wait_ge(pr.sem("v_n"), 7 * h + 7)
                pr.inc("pe_nb", eng.matmul(ps_nb[:, 0:1], ones[0:1, :], row_m[0:1, 4:5],
                                           start=True, stop=True))
                for ktc in range(32):
                    if ktc == 0:
                        eng.wait_ge(pr.sem("v_kc"), 2 * h + 2)
                        eng.wait_ge(pr.sem("v_rr"), 2 * h)
                        eng.wait_ge(pr.sem("v_ao"), h)
                        if h == 0:
                            eng.wait_ge(pr.sem("v_qf"), 16)
                    eng.wait_ge(pr.sem("s_exp"), 32 * h + ktc)
                    inst = None
                    for tc in range(2):
                        ts = slice(tc * 512, (tc + 1) * 512)
                        inst = eng.matmul(ps_q[:, ts], kb[:, ktc * 128:(ktc + 1) * 128],
                                          qf[:, h * TOK + tc * 512: h * TOK + (tc + 1) * 512],
                                          start=True, stop=True)
                    pr.inc("pe_s", inst)
                    lc = ktc % 8
                    eng.wait_ge(pr.sem("s_exp"), 32 * h + ktc + 1)
                    eng.wait_ge(pr.sem("v_vc"), 2 * h + (1 if ktc < 16 else 2))
                    for tc in range(2):
                        ts = slice(tc * 512, (tc + 1) * 512)
                        eng.matmul(ps_row[0:1, ts], onebv[:, 0:1],
                                   ptq[:, lc * TOK + tc * 512: lc * TOK + (tc + 1) * 512],
                                   start=(ktc == 0), stop=(ktc == 31))
                    inst = None
                    for tc in range(2):
                        ts = slice(tc * 512, (tc + 1) * 512)
                        inst = eng.matmul(ps_o[:, ts], vb[:, ktc * 128:(ktc + 1) * 128],
                                          ptq[:, lc * TOK + tc * 512: lc * TOK + (tc + 1) * 512],
                                          start=(ktc == 0), stop=(ktc == 31))
                    pr.inc("pe_pv", inst)
                eng.wait_ge(pr.sem("v_rr"), 2 * h + 1)
                eng.wait_ge(pr.sem("s_exp"), 32 * (h + 1))
                inst = None
                for sc in range(2):
                    os_ = slice(sc * 512, (sc + 1) * 512)
                    inst = eng.matmul(ps_q[:, os_], ones[0:1, :], srow[0:1, os_],
                                      start=True, stop=True)
                pr.inc("pe_rr", inst)

            @blk.vector
            def _(eng):
                for half in range(2):
                    eng.wait_ge(pr.sem("d_kv"), 64 * h + 16 * (half + 1))
                    if h > 0 and half == 0:
                        eng.wait_ge(pr.sem("pe_pv"), 32 * h)
                    pr.inc("v_kc", eng.tensor_scalar(
                        kb[:, half * 2048:(half + 1) * 2048], stg[:], SCALE_ATT, None, ALU.mult))
                    for c in range(2):
                        i = 2 * half + c
                        eng.wait_ge(pr.sem("pe_n"), 5 * h + i + 1)
                        if i == 0:
                            pr.inc("v_n", eng.reduce_max(row_m[0:1, 0:1], ps_n[0:1, 0:TOK],
                                                         axis=AX.X))
                        else:
                            pr.chain(eng, "vq", eng.reduce_max(row_m[0:1, 5:6], ps_n[0:1, 0:TOK],
                                                               axis=AX.X))
                            eng.wait_ge(pr.sem("v_n"), 7 * h + i)
                            pr.inc("v_n", eng.tensor_max(row_m[0:1, 0:1], row_m[0:1, 0:1],
                                                         row_m[0:1, 5:6]))
                for half in range(2):
                    eng.wait_ge(pr.sem("d_kv"), 64 * h + 32 + 16 * (half + 1))
                    if h > 0 and half == 0:
                        eng.wait_ge(pr.sem("pe_pv"), 32 * h)
                    pr.inc("v_vc", eng.tensor_copy(vb[:, half * 2048:(half + 1) * 2048], stg[:]))
                eng.wait_ge(pr.sem("pe_n"), 5 * h + 5)
                pr.inc("v_n", eng.reduce_max(row_m[0:1, 1:2], ps_n[0:1, 0:TOK], axis=AX.X))
                eng.wait_ge(pr.sem("v_n"), 7 * h + 5)
                pr.inc("v_n", eng.tensor_mul(row_m[0:1, 2:3], row_m[0:1, 0:1], row_m[0:1, 1:2]))
                eng.wait_ge(pr.sem("s_rt2"), h + 1)
                pr.inc("v_n", eng.tensor_scalar(row_m[0:1, 4:5], row_m[0:1, 3:4], -1.0, None,
                                                ALU.mult))
                eng.wait_ge(pr.sem("pe_nb"), h + 1)
                pr.inc("v_bb", eng.tensor_copy(negb[:], ps_nb[:, 0:1]))
                eng.wait_ge(pr.sem("pe_pv"), 32 * (h + 1))
                pr.inc("v_rr", eng.reciprocal(srow[0:1, 0:TOK], ps_row[0:1, 0:TOK]))
                eng.wait_ge(pr.sem("pe_rr"), h + 1)
                pr.inc("v_rr", eng.tensor_copy(rrb[:], ps_q[:]))
                eng.wait_ge(pr.sem("v_rr"), 2 * h + 2)
                pr.inc("v_ao", eng.tensor_mul(xab[:, h * TOK:(h + 1) * TOK], ps_o[:], rrb[:]))

    for od in range(16):
        with nc.Block() as blk:
            @blk.sync
            def _(eng):
                if od == 0:
                    eng.wait_ge(pr.sem("v_vc"), 32)
                else:
                    eng.wait_ge(pr.sem("v_wo"), 2 * od)
                pr.inc("d_wo", eng.dma_start(
                    stg[:].rearrange("p (kc f) -> p kc f", kc=16),
                    woT[:, od * 128:(od + 1) * 128].rearrange("(kc kp) f -> kp kc f", kp=128)), 16)

            @blk.vector
            def _(eng):
                eng.wait_ge(pr.sem("d_wo"), 16 * (od + 1))
                if od > 0:
                    eng.wait_ge(pr.sem("pe_w"), od)
                pr.inc("v_wo", eng.tensor_copy(w_b[:], stg[:]))
                eng.wait_ge(pr.sem("pe_w"), od + 1)
                pr.inc("v_wo", eng.tensor_add(xres[:, od * TOK:(od + 1) * TOK],
                                              xres[:, od * TOK:(od + 1) * TOK], ps_q[:]))

            @blk.tensor
            def _(eng):
                eng.wait_ge(pr.sem("v_wo"), 2 * od + 1)
                if od == 0:
                    eng.wait_ge(pr.sem("v_ao"), 16)
                    eng.wait_ge(pr.sem("v_rr"), 32)
                else:
                    eng.wait_ge(pr.sem("v_wo"), 2 * od)
                inst = None
                for tc in range(2):
                    ts = slice(tc * 512, (tc + 1) * 512)
                    for kc in range(16):
                        inst = eng.matmul(ps_q[:, ts], w_b[:, kc * 128:(kc + 1) * 128],
                                          xab[:, kc * TOK + tc * 512: kc * TOK + (tc + 1) * 512],
                                          start=(kc == 0), stop=(kc == 15))
                pr.inc("pe_w", inst)

    with nc.Block() as blk:
        @blk.sync
        def _(eng):
            pr.wait(eng, "v_wo")
            for j in range(NF):
                pr.inc("d_out", eng.dma_start(xo[j * 128:(j + 1) * 128, :],
                                              xres[:, j * TOK:(j + 1) * TOK]), 16)
            pr.wait(eng, "d_out")
    return nc


def build_l1ffn():
    pr = Pr()
    nc = pr.nc
    xf = nc.dram_tensor("xf", [D, TOK], F32, kind="ExternalInput").ap()
    gwr = nc.dram_tensor("gwr", [128, 128], F32, kind="ExternalInput").ap()
    weg = nc.dram_tensor("weg", [8, D, 1408], F32, kind="ExternalInput").ap()
    weu = nc.dram_tensor("weu", [8, D, 1408], F32, kind="ExternalInput").ap()
    wed = nc.dram_tensor("wed", [8, 1408, D], F32, kind="ExternalInput").ap()
    swg = nc.dram_tensor("swg", [D, 2816], F32, kind="ExternalInput").ap()
    swu = nc.dram_tensor("swu", [D, 2816], F32, kind="ExternalInput").ap()
    swd = nc.dram_tensor("swd", [2816, D], F32, kind="ExternalInput").ap()
    onesf = nc.dram_tensor("onesf", [128, 128], F32, kind="ExternalInput").ap()
    identf = nc.dram_tensor("identf", [128, 128], F32, kind="ExternalInput").ap()
    cstf = nc.dram_tensor("cstf", [128, 8], F32, kind="ExternalInput").ap()
    xo = nc.dram_tensor("xo", [D, TOK], F32, kind="ExternalOutput").ap()

    NFT = 11
    xres = nc.alloc_sbuf_tensor("xres", [128, NF * TOK], F32)
    xln = nc.alloc_sbuf_tensor("xln", [128, NF * TOK], BF16)
    sq = nc.alloc_sbuf_tensor("sq", [128, 2 * TOK], F32)
    row_m = nc.alloc_sbuf_tensor("row_m", [1, TOK], F32)
    row_v = nc.alloc_sbuf_tensor("row_v", [1, TOK], F32)
    ones = nc.alloc_sbuf_tensor("ones", [128, 128], F32)
    ident = nc.alloc_sbuf_tensor("ident", [128, 128], F32)
    cst = nc.alloc_sbuf_tensor("cst", [128, 8], F32)
    gwb = nc.alloc_sbuf_tensor("gwb", [128, 128], BF16)
    h = nc.alloc_sbuf_tensor("h", [128, NFT * TOK], BF16)
    hg = nc.alloc_sbuf_tensor("hg", [128, TOK], BF16)
    wstg = nc.alloc_sbuf_tensor("wstg", [128, 2048], F32)
    wgb = nc.alloc_sbuf_tensor("wgb", [128, 2048], BF16)
    wub = nc.alloc_sbuf_tensor("wub", [128, 2048], BF16)
    wd_blk = nc.alloc_sbuf_tensor("wd_blk", [128, NFT * 1024], BF16)
    comb_tm = nc.alloc_sbuf_tensor("comb_tm", [128, 64], F32)
    probs = nc.alloc_sbuf_tensor("probs", [128, 8], F32)
    scs = nc.alloc_sbuf_tensor("scs", [128, 8], F32)
    colm = nc.alloc_sbuf_tensor("colm", [128, 4], F32)
    crows = nc.alloc_sbuf_tensor("crows", [1, TOK], F32)
    combB = nc.alloc_sbuf_tensor("combB", [128, TOK], F32)
    tmpb = nc.alloc_sbuf_tensor("tmpb", [128, TOK], F32)

    pool = nc.alloc_psum_tensor("pool", [128, 4096], F32)
    ps_sum = pool[0:1, 0:TOK]
    ps_sq = pool[0:1, TOK:2 * TOK]
    ps_bc = pool[:, 2 * TOK:4 * TOK]
    ps_l = pool[:, 0:8]
    ps_g = pool[:, 0:TOK]
    ps_u = pool[:, TOK:2 * TOK]
    ps_ct = pool[0:1, 3 * TOK:3 * TOK + 128]
    ps_cb = pool[:, 3 * TOK:4 * TOK]

    with nc.Block() as blk:
        @blk.sync
        def _(eng):
            for j in range(NF):
                pr.inc("d_x", eng.dma_start(xres[:, j * TOK:(j + 1) * TOK],
                                            xf[j * 128:(j + 1) * 128, :]), 16)
            pr.inc("d_c", eng.dma_start(ones[:], onesf), 16)
            pr.inc("d_c", eng.dma_start(ident[:], identf), 16)
            pr.inc("d_c", eng.dma_start(cst[:], cstf), 16)

    with nc.Block() as blk:
        @blk.sync
        def _(eng):
            pr.inc("d_gw", eng.dma_start(wstg[:, 0:128], gwr), 16)

        @blk.vector
        def _(eng):
            pr.wait(eng, "d_gw")
            pr.inc("v_gw", eng.tensor_copy(gwb[:], wstg[:, 0:128]))

    emit_ln(pr, "ln2", xres[:], xln[:], ones[:], sq[:], row_m[:], row_v[:],
            ps_sum, ps_sq, ps_bc, TOK,
            [("d_x", pr.val("d_x")), ("d_c", pr.val("d_c"))], cst[0:1, 0:1])

    for tt in range(8):
        with nc.Block() as blk:
            @blk.tensor
            def _(eng):
                if tt == 0:
                    pr.wait(eng, "ln2dn")
                    pr.wait(eng, "v_gw")
                eng.wait_ge(pr.sem("v_r"), 2 * tt)
                inst = None
                for kc in range(16):
                    inst = eng.matmul(ps_l[:, 0:8], xln[:, kc * TOK + tt * 128: kc * TOK + (tt + 1) * 128],
                                      gwb[:, kc * 8:(kc + 1) * 8],
                                      start=(kc == 0), stop=(kc == 15))
                pr.inc("pe_r", inst)

            @blk.scalar
            def _(eng):
                eng.wait_ge(pr.sem("v_r"), 2 * tt + 1)
                pr.inc("s_r", eng.activation(probs[:], ps_l[:, 0:8], AF.Exp,
                                             bias=colm[:, 1:2]))

            @blk.vector
            def _(eng):
                eng.wait_ge(pr.sem("pe_r"), tt + 1)
                pr.chain(eng, "vq", eng.reduce_max(colm[:, 0:1], ps_l[:, 0:8], axis=AX.X))
                pr.inc("v_r", eng.tensor_scalar(colm[:, 1:2], colm[:, 0:1], -1.0, None, ALU.mult))
                eng.wait_ge(pr.sem("s_r"), tt + 1)
                pr.chain(eng, "vq", eng.reduce_sum(colm[:, 0:1], probs[:], axis=AX.X))
                pr.chain(eng, "vq", eng.reciprocal(colm[:, 1:2], colm[:, 0:1]))
                pr.chain(eng, "vq", eng.tensor_scalar(scs[:], probs[:], colm[:, 1:2], None, ALU.mult))
                pr.chain(eng, "vq", eng.reduce_max(colm[:, 0:1], scs[:], axis=AX.X))
                pr.chain(eng, "vq", eng.tensor_scalar(probs[:], scs[:], colm[:, 0:1], None, ALU.is_ge))
                pr.chain(eng, "vq", eng.scalar_tensor_tensor(probs[:], probs[:], -1e30, scs[:],
                                                             ALU.mult, ALU.add))
                pr.chain(eng, "vq", eng.reduce_max(colm[:, 2:3], probs[:], axis=AX.X))
                pr.chain(eng, "vq", eng.tensor_scalar(probs[:], scs[:], colm[:, 2:3], None, ALU.is_ge))
                pr.inc("v_r", eng.tensor_mul(comb_tm[:, tt * 8:(tt + 1) * 8], scs[:], probs[:]))

    for r in range(10):
        is_exp = r < 8
        e = r if is_exp else 0
        sh = 0 if is_exp else (r - 8)

        def g_slab(ft):
            if is_exp:
                return weg[e, :, ft * 128:(ft + 1) * 128]
            return swg[:, sh * 1408 + ft * 128: sh * 1408 + (ft + 1) * 128]

        def u_slab(ft):
            if is_exp:
                return weu[e, :, ft * 128:(ft + 1) * 128]
            return swu[:, sh * 1408 + ft * 128: sh * 1408 + (ft + 1) * 128]

        def d_slab(fc):
            if is_exp:
                return wed[e, fc * 128:(fc + 1) * 128, :]
            return swd[sh * 1408 + fc * 128: sh * 1408 + (fc + 1) * 128, :]

        if is_exp:
            with nc.Block() as blk:
                @blk.tensor
                def _(eng):
                    eng.wait_ge(pr.sem("v_ed"), 16 * r)
                    eng.wait_ge(pr.sem("v_r"), 16)
                    for tt in range(8):
                        if tt > 0:
                            eng.wait_ge(pr.sem("v_ct"), 8 * r + tt)
                        pr.inc("pe_ct", eng.transpose(ps_ct[0:1, 0:128],
                                                      comb_tm[:, tt * 8 + e:tt * 8 + e + 1],
                                                      ident[:]))

                @blk.vector
                def _(eng):
                    for tt in range(8):
                        eng.wait_ge(pr.sem("pe_ct"), 8 * r + tt + 1)
                        pr.inc("v_ct", eng.tensor_copy(
                            crows[0:1, tt * 128:(tt + 1) * 128], ps_ct[0:1, 0:128]))

            with nc.Block() as blk:
                @blk.tensor
                def _(eng):
                    eng.wait_ge(pr.sem("v_ct"), 8 * (r + 1))
                    inst = None
                    for sc_ in range(2):
                        os_ = slice(sc_ * 512, (sc_ + 1) * 512)
                        inst = eng.matmul(ps_cb[:, os_], ones[0:1, :], crows[0:1, os_],
                                          start=True, stop=True)
                    pr.inc("pe_cb", inst)

                @blk.vector
                def _(eng):
                    eng.wait_ge(pr.sem("pe_cb"), r + 1)
                    pr.inc("v_cb", eng.tensor_copy(combB[:], ps_cb[:]))

        for ft in range(NFT):
            ftg = r * NFT + ft
            with nc.Block() as blk:
                @blk.sync
                def _(eng):
                    if ftg > 0:
                        eng.wait_ge(pr.sem("v_wc"), 2 * ftg)
                    if ft == 0 and r > 0:
                        eng.wait_ge(pr.sem("v_wdc"), 22 * r)
                    pr.inc("d_we", eng.dma_start(
                        wstg[:].rearrange("p (kc f) -> p kc f", kc=16),
                        g_slab(ft).rearrange("(kc kp) f -> kp kc f", kp=128)), 16)
                    eng.wait_ge(pr.sem("v_wc"), 2 * ftg + 1)
                    pr.inc("d_we", eng.dma_start(
                        wstg[:].rearrange("p (kc f) -> p kc f", kc=16),
                        u_slab(ft).rearrange("(kc kp) f -> kp kc f", kp=128)), 16)

                @blk.vector
                def _(eng):
                    eng.wait_ge(pr.sem("d_we"), 32 * ftg + 16)
                    eng.wait_ge(pr.sem("pe_egu"), ftg)
                    pr.inc("v_wc", eng.tensor_copy(wgb[:], wstg[:]))
                    eng.wait_ge(pr.sem("d_we"), 32 * ftg + 32)
                    pr.inc("v_wc", eng.tensor_copy(wub[:], wstg[:]))
                    eng.wait_ge(pr.sem("pe_egu"), ftg + 1)
                    eng.wait_ge(pr.sem("s_sil"), ftg + 1)
                    if ft == 0 and r > 0:
                        eng.wait_ge(pr.sem("pe_ed"), 4 * r)
                    pr.inc("v_h", eng.tensor_mul(h[:, ft * TOK:(ft + 1) * TOK], hg[:], ps_u[:]))

                @blk.scalar
                def _(eng):
                    eng.wait_ge(pr.sem("pe_eg"), ftg + 1)
                    eng.wait_ge(pr.sem("v_h"), ftg)
                    pr.inc("s_sil", eng.activation(hg[:], ps_g[:], AF.Silu))

                @blk.tensor
                def _(eng):
                    eng.wait_ge(pr.sem("v_wc"), 2 * ftg + 1)
                    if ftg == 0:
                        pr.wait(eng, "ln2dn")
                        eng.wait_ge(pr.sem("v_r"), 16)
                    eng.wait_ge(pr.sem("s_sil"), ftg)
                    eng.wait_ge(pr.sem("v_h"), ftg)
                    inst = None
                    for tc in range(2):
                        ts = slice(tc * 512, (tc + 1) * 512)
                        for kc in range(16):
                            inst = eng.matmul(ps_g[:, ts], wgb[:, kc * 128:(kc + 1) * 128],
                                              xln[:, kc * TOK + tc * 512: kc * TOK + (tc + 1) * 512],
                                              start=(kc == 0), stop=(kc == 15))
                    pr.inc("pe_eg", inst)
                    eng.wait_ge(pr.sem("v_wc"), 2 * ftg + 2)
                    for tc in range(2):
                        ts = slice(tc * 512, (tc + 1) * 512)
                        for kc in range(16):
                            inst = eng.matmul(ps_u[:, ts], wub[:, kc * 128:(kc + 1) * 128],
                                              xln[:, kc * TOK + tc * 512: kc * TOK + (tc + 1) * 512],
                                              start=(kc == 0), stop=(kc == 15))
                    pr.inc("pe_egu", inst)

        for dh in range(2):
            with nc.Block() as blk:
                @blk.sync
                def _(eng):
                    for fc in range(NFT):
                        fcg = 22 * r + dh * NFT + fc
                        if fcg > 0:
                            eng.wait_ge(pr.sem("v_wdc"), fcg)
                        if fc == 0 and dh == 0:
                            eng.wait_ge(pr.sem("v_wc"), 22 * (r + 1))
                        pr.inc("d_wd2", eng.dma_start(wstg[:], d_slab(fc)), 16)

                @blk.vector
                def _(eng):
                    for fc in range(NFT):
                        fcg = 22 * r + dh * NFT + fc
                        eng.wait_ge(pr.sem("d_wd2"), 16 * (fcg + 1))
                        if fc == 0:
                            eng.wait_ge(pr.sem("pe_ed"), 4 * r + 2 * dh)
                        pr.inc("v_wdc", eng.tensor_copy(
                            wd_blk[:, fc * 1024:(fc + 1) * 1024],
                            wstg[:, dh * 1024:(dh + 1) * 1024]))

            for dgl in range(2):
                dg = 2 * dh + dgl
                with nc.Block() as blk:
                    @blk.tensor
                    def _(eng):
                        eng.wait_ge(pr.sem("v_wdc"), 22 * r + NFT * (dh + 1))
                        eng.wait_ge(pr.sem("v_h"), NFT * (r + 1))
                        eng.wait_ge(pr.sem("v_ed"), 16 * r + 4 * dg)
                        if is_exp and dg == 3:
                            eng.wait_ge(pr.sem("v_cb"), r + 1)
                        inst = None
                        for dl in range(4):
                            dt = dg * 4 + dl
                            dloc = dt - 8 * dh
                            psd = pool[:, dl * TOK:(dl + 1) * TOK]
                            for fc in range(NFT):
                                for tc in range(2):
                                    ts = slice(tc * 512, (tc + 1) * 512)
                                    inst = eng.matmul(
                                        psd[:, ts],
                                        wd_blk[:, fc * 1024 + dloc * 128: fc * 1024 + (dloc + 1) * 128],
                                        h[:, fc * TOK + tc * 512: fc * TOK + (tc + 1) * 512],
                                        start=(fc == 0), stop=(fc == NFT - 1))
                        pr.inc("pe_ed", inst)

                    @blk.vector
                    def _(eng):
                        eng.wait_ge(pr.sem("pe_ed"), 4 * r + dg + 1)
                        if is_exp:
                            eng.wait_ge(pr.sem("v_cb"), r + 1)
                        for dl in range(4):
                            dt = dg * 4 + dl
                            psd = pool[:, dl * TOK:(dl + 1) * TOK]
                            if is_exp:
                                eng.wait_ge(pr.sem("v_ed"), 16 * r + 4 * dg + dl)
                                pr.chain(eng, "vq", eng.tensor_mul(tmpb[:], psd, combB[:]))
                                pr.inc("v_ed", eng.tensor_add(xres[:, dt * TOK:(dt + 1) * TOK],
                                                              xres[:, dt * TOK:(dt + 1) * TOK], tmpb[:]))
                            else:
                                pr.inc("v_ed", eng.scalar_tensor_tensor(
                                    xres[:, dt * TOK:(dt + 1) * TOK], psd, 1.0,
                                    xres[:, dt * TOK:(dt + 1) * TOK], ALU.mult, ALU.add))

    with nc.Block() as blk:
        @blk.sync
        def _(eng):
            pr.wait(eng, "v_ed")
            for j in range(NF):
                pr.inc("d_out", eng.dma_start(xo[j * 128:(j + 1) * 128, :],
                                              xres[:, j * TOK:(j + 1) * TOK]), 16)
            pr.wait(eng, "d_out")
    return nc


def _l0a_maps(inputs, x):
    winT = np.ascontiguousarray(np.asarray(inputs["L0_attn_in_proj_w"], np.float32).T)
    conv_w = np.asarray(inputs["L0_attn_conv_w"], np.float32)
    cwf = np.ascontiguousarray(conv_w.reshape(32, 128, 4).transpose(1, 0, 2).reshape(128, 128))
    conv_b = np.asarray(inputs["L0_attn_conv_b"], np.float32)
    cbf = np.ascontiguousarray(conv_b.reshape(32, 128).T)
    wxT = np.ascontiguousarray(np.asarray(inputs["L0_attn_x_proj_w"], np.float32).T)
    maps = []
    for c in range(NCORES):
        seq, half = c // 2, c % 2
        t0 = half * TOK
        if half == 0:
            xe = np.concatenate([np.zeros((3, D), np.float32), x[seq, 0:TOK]], axis=0)
        else:
            xe = x[seq, t0 - 3:t0 + TOK]
        xfh = np.ascontiguousarray(xe.T)
        maps.append(dict(xf=xfh, winT=winT, cwf=cwf, cbf=cbf, wxT=wxT,
                         onesf=_ONES, cstf=_CST))
    return maps


def _l0b_maps(inputs, xf_percore, a_outs, sin_percore):
    dtwT = np.ascontiguousarray(np.asarray(inputs["L0_attn_dt_proj_w"], np.float32).T)
    dtb = np.ascontiguousarray(np.asarray(inputs["L0_attn_dt_proj_b"], np.float32).reshape(64, 1))
    alog = np.ascontiguousarray(np.asarray(inputs["L0_attn_A_log"], np.float32).reshape(64, 1))
    Dv = np.asarray(inputs["L0_attn_D"], np.float32)
    ddc = np.zeros((128, 32), np.float32)
    for J in range(32):
        ddc[0:64, J] = Dv[2 * J]
        ddc[64:128, J] = Dv[2 * J + 1]
    woutT = np.ascontiguousarray(np.asarray(inputs["L0_attn_out_proj_w"], np.float32).T)
    eye = np.eye(64, dtype=BF)
    onescol = np.ones((64, 64), BF)
    repa = np.ascontiguousarray(np.kron(np.eye(64), np.ones((1, 64))).astype(BF))
    trilq = np.triu(np.ones((64, 64), np.float32))
    tril2 = np.ascontiguousarray(np.tile(trilq, (1, 64)).astype(BF))
    repi = np.ascontiguousarray(np.tile(np.eye(64, dtype=np.float32), (1, 64)).astype(BF))
    repd = np.zeros((64, 4096), BF)
    for J in range(32):
        repd[2 * J, J * 128:J * 128 + 64] = 1
        repd[2 * J + 1, J * 128 + 64:J * 128 + 128] = 1
    trius1 = np.tril(np.ones((64, 64), np.float32), -1)
    trius = np.ascontiguousarray(trius1.astype(BF))
    maskl1 = np.tril(np.ones((64, 64), np.float32))
    maskls = np.ascontiguousarray(maskl1.astype(BF))
    maps = {}
    for c in a_outs:
        xfm_p = np.zeros((4096, 1088), BF)
        xfm_p[:, 0:1024] = a_outs[c]["xfm"]
        bcdt_p = np.zeros((320, 1088), BF)
        bcdt_p[:, 0:1024] = a_outs[c]["bcdt"]
        maps[c] = dict(xf=xf_percore[c], xfm=xfm_p, bcdt=bcdt_p, zsf=a_outs[c]["zsf"],
                       dtwT=dtwT, dtb=dtb, alog=alog, ddf=ddc, woutT=woutT,
                       sinf=sin_percore[c], onesf=_ONES, repaf=repa, trilf=tril2,
                       repif=repi, repdf=repd, triusf=trius, maskls=maskls, cstf=_CST)
    return maps


def _moe_maps(inputs, xf_percore):
    gw = np.asarray(inputs["L1_ffn_gate_w"], np.float32)
    gwr = np.ascontiguousarray(gw.reshape(16, 128, 8).transpose(1, 0, 2).reshape(128, 128))
    weg = np.ascontiguousarray(np.asarray(inputs["L1_ffn_w_gate"], np.float32))
    weu = np.ascontiguousarray(np.asarray(inputs["L1_ffn_w_up"], np.float32))
    wed = np.ascontiguousarray(np.asarray(inputs["L1_ffn_w_down"], np.float32))
    swg = np.ascontiguousarray(np.asarray(inputs["L1_ffn_sw_gate"], np.float32))
    swu = np.ascontiguousarray(np.asarray(inputs["L1_ffn_sw_up"], np.float32))
    swd = np.ascontiguousarray(np.asarray(inputs["L1_ffn_sw_down"], np.float32))
    return [dict(xf=xf_percore[c], gwr=gwr, weg=weg, weu=weu, wed=wed,
                 swg=swg, swu=swu, swd=swd, onesf=_ONES, identf=_IDENT, cstf=_CST)
            for c in range(NCORES)]


TOKH = 1027


def build_l0a():
    pr = Pr()
    nc = pr.nc
    xf = nc.dram_tensor("xf", [D, TOKH], F32, kind="ExternalInput").ap()
    winT = nc.dram_tensor("winT", [D, 8192], F32, kind="ExternalInput").ap()
    cwf = nc.dram_tensor("cwf", [128, 128], F32, kind="ExternalInput").ap()
    cbf = nc.dram_tensor("cbf", [128, 32], F32, kind="ExternalInput").ap()
    wxT = nc.dram_tensor("wxT", [4096, 4416], F32, kind="ExternalInput").ap()
    onesf = nc.dram_tensor("onesf", [128, 128], F32, kind="ExternalInput").ap()
    cstf = nc.dram_tensor("cstf", [128, 8], F32, kind="ExternalInput").ap()
    xfm = nc.dram_tensor("xfm", [4096, TOK], BF16, kind="ExternalOutput").ap()
    bcdt = nc.dram_tensor("bcdt", [320, TOK], BF16, kind="ExternalOutput").ap()
    zsf = nc.dram_tensor("zsf", [4096, TOK], BF16, kind="ExternalOutput").ap()

    xln = nc.alloc_sbuf_tensor("xln", [128, NF * TOKH], BF16)
    u = nc.alloc_sbuf_tensor("u", [128, 32 * TOK], BF16)
    xstg = nc.alloc_sbuf_tensor("xstg", [128, 2 * TOKH], F32)
    sq = nc.alloc_sbuf_tensor("sq", [128, 2 * TOKH], F32)
    row_m = nc.alloc_sbuf_tensor("row_m", [1, TOKH], F32)
    row_v = nc.alloc_sbuf_tensor("row_v", [1, TOKH], F32)
    ones = nc.alloc_sbuf_tensor("ones", [128, 128], F32)
    cst = nc.alloc_sbuf_tensor("cst", [128, 8], F32)
    cw = nc.alloc_sbuf_tensor("cw", [128, 128], F32)
    cb = nc.alloc_sbuf_tensor("cb", [128, 32], F32)
    wstg = nc.alloc_sbuf_tensor("wstg", [128, 2048], F32)
    w_b = nc.alloc_sbuf_tensor("w_b", [128, 4096], BF16)
    xb = nc.alloc_sbuf_tensor("xb", [128, TOKH], F32)
    acc1 = nc.alloc_sbuf_tensor("acc1", [128, TOK], F32)
    acc2 = nc.alloc_sbuf_tensor("acc2", [128, TOK], F32)
    ostg = nc.alloc_sbuf_tensor("ostg", [128, 2 * TOK], BF16)

    pool = nc.alloc_psum_tensor("pool", [128, 4096], F32)
    ps_sum = pool[0:1, 0:TOKH]
    ps_sq = pool[0:1, 1536:1536 + TOKH]
    ps_bcm = pool[:, 0:TOKH]
    ps_bcr = pool[:, 1536:1536 + TOKH]
    ps_m = pool[:, 0:TOKH]
    ps_x = pool[:, 0:TOK]

    with nc.Block() as blk:
        @blk.sync
        def _(eng):
            pr.inc("d_c", eng.dma_start(ones[:], onesf), 16)
            pr.inc("d_c", eng.dma_start(cst[:], cstf), 16)
            pr.inc("d_c", eng.dma_start(cw[:], cwf), 16)
            pr.inc("d_c", eng.dma_start(cb[:], cbf), 16)

    with nc.Block() as blk:
        @blk.sync
        def _(eng):
            for j in range(NF):
                if j >= 2:
                    eng.wait_ge(pr.sem("lnA_pe"), j - 1)
                pr.inc("d_lx%d" % (j % 2), eng.dma_start(
                    xstg[:, (j % 2) * TOKH:(j % 2 + 1) * TOKH],
                    xf[j * 128:(j + 1) * 128, :]), 16)

        @blk.scalar
        def _(eng):
            for j in range(NF):
                eng.wait_ge(pr.sem("d_lx%d" % (j % 2)), 16 * (j // 2 + 1))
                if j >= 2:
                    eng.wait_ge(pr.sem("lnA_pe"), j - 1)
                pr.inc("lnA_sq", eng.activation(sq[:, (j % 2) * TOKH:(j % 2 + 1) * TOKH],
                                                xstg[:, (j % 2) * TOKH:(j % 2 + 1) * TOKH],
                                                AF.Square))

        @blk.tensor
        def _(eng):
            pr.wait(eng, "d_c")
            for j in range(NF):
                eng.wait_ge(pr.sem("d_lx%d" % (j % 2)), 16 * (j // 2 + 1))
                xt = xstg[:, (j % 2) * TOKH:(j % 2 + 1) * TOKH]
                for c in range(3):
                    c0, c1 = c * 512, min((c + 1) * 512, TOKH)
                    eng.matmul(ps_sum[0:1, c0:c1], ones[:, 0:1], xt[:, c0:c1],
                               start=(j == 0), stop=(j == NF - 1))
                eng.wait_ge(pr.sem("lnA_sq"), j + 1)
                sqt = sq[:, (j % 2) * TOKH:(j % 2 + 1) * TOKH]
                inst = None
                for c in range(3):
                    c0, c1 = c * 512, min((c + 1) * 512, TOKH)
                    inst = eng.matmul(ps_sq[0:1, c0:c1], ones[:, 0:1], sqt[:, c0:c1],
                                      start=(j == 0), stop=(j == NF - 1))
                pr.inc("lnA_pe", inst)

    with nc.Block() as blk:
        @blk.vector
        def _(eng):
            eng.wait_ge(pr.sem("lnA_pe"), NF)
            pr.chain(eng, "vq", eng.tensor_scalar(row_m[0:1, :], ps_sum[0:1, :], 1.0 / D, None, ALU.mult))
            pr.chain(eng, "vq", eng.tensor_scalar(row_v[0:1, :], ps_sq[0:1, :], 1.0 / D, None, ALU.mult))
            pr.chain(eng, "vq", eng.tensor_mul(sq[0:1, 0:TOKH], row_m[0:1, :], row_m[0:1, :]))
            pr.inc("ln_dv", eng.tensor_sub(row_v[0:1, :], row_v[0:1, :], sq[0:1, 0:TOKH]))

        @blk.scalar
        def _(eng):
            pr.wait(eng, "ln_dv")
            pr.wait(eng, "d_c")
            pr.inc("ln_rt", eng.activation(row_v[0:1, :], row_v[0:1, :], AF.Sqrt,
                                           bias=cst[0:1, 0:1]))

    with nc.Block() as blk:
        @blk.vector
        def _(eng):
            pr.wait(eng, "ln_rt")
            pr.inc("ln_rt", eng.reciprocal(row_v[0:1, :], row_v[0:1, :]))

        @blk.tensor
        def _(eng):
            eng.wait_ge(pr.sem("ln_rt"), 2)
            inst = None
            for c in range(3):
                c0, c1 = c * 512, min((c + 1) * 512, TOKH)
                eng.matmul(ps_bcm[:, c0:c1], ones[0:1, :], row_m[0:1, c0:c1],
                           start=True, stop=True)
                inst = eng.matmul(ps_bcr[:, c0:c1], ones[0:1, :],
                                  row_v[0:1, c0:c1], start=True, stop=True)
            pr.inc("ln_bc", inst)

    with nc.Block() as blk:
        @blk.sync
        def _(eng):
            for j in range(NF):
                if j >= 2:
                    eng.wait_ge(pr.sem("lnC_dv"), j - 1)
                pr.inc("d_ly%d" % (j % 2), eng.dma_start(
                    xstg[:, (j % 2) * TOKH:(j % 2 + 1) * TOKH],
                    xf[j * 128:(j + 1) * 128, :]), 16)

        @blk.vector
        def _(eng):
            pr.wait(eng, "ln_bc")
            for j in range(NF):
                eng.wait_ge(pr.sem("d_ly%d" % (j % 2)), 16 * (j // 2 + 1))
                tmp = sq[:, (j % 2) * TOKH:(j % 2 + 1) * TOKH]
                pr.chain(eng, "vq", eng.tensor_sub(tmp, xstg[:, (j % 2) * TOKH:(j % 2 + 1) * TOKH],
                                                   ps_bcm[:]))
                pr.inc("lnC_dv", eng.tensor_mul(xln[:, j * TOKH:(j + 1) * TOKH], tmp,
                                                ps_bcr[:]))

    for j in range(32):
        with nc.Block() as blk:
            @blk.sync
            def _(eng):
                if j > 0:
                    eng.wait_ge(pr.sem("v_wc"), j)
                pr.inc("d_wi", eng.dma_start(
                    wstg[:].rearrange("p (kc f) -> p kc f", kc=16),
                    winT[:, j * 128:(j + 1) * 128].rearrange("(kc kp) f -> kp kc f", kp=128)), 16)

            @blk.vector
            def _(eng):
                eng.wait_ge(pr.sem("d_wi"), 16 * (j + 1))
                if j > 0:
                    eng.wait_ge(pr.sem("pe_ip"), j)
                pr.inc("v_wc", eng.tensor_copy(w_b[:, 0:2048], wstg[:]))
                if j == 0:
                    pr.wait(eng, "d_c")
                eng.wait_ge(pr.sem("pe_ip"), j + 1)
                pr.chain(eng, "vq", eng.tensor_copy(xb[:], ps_m[:]))
                pr.chain(eng, "vq", eng.tensor_scalar(acc1[:], xb[:, 0:TOK], cw[:, 4 * (j % 32):4 * (j % 32) + 1], None, ALU.mult))
                pr.chain(eng, "vq", eng.scalar_tensor_tensor(acc2[:], xb[:, 1:1 + TOK], cw[:, 4 * (j % 32) + 1:4 * (j % 32) + 2], acc1[:], ALU.mult, ALU.add))
                pr.chain(eng, "vq", eng.scalar_tensor_tensor(acc1[:], xb[:, 2:2 + TOK], cw[:, 4 * (j % 32) + 2:4 * (j % 32) + 3], acc2[:], ALU.mult, ALU.add))
                pr.inc("v_cv", eng.scalar_tensor_tensor(acc2[:], xb[:, 3:3 + TOK], cw[:, 4 * (j % 32) + 3:4 * (j % 32) + 4], acc1[:], ALU.mult, ALU.add))

            @blk.scalar
            def _(eng):
                if j == 0:
                    pr.wait(eng, "d_c")
                eng.wait_ge(pr.sem("v_cv"), j + 1)
                pr.inc("s_u", eng.activation(u[:, j * TOK:(j + 1) * TOK], acc2[:], AF.Silu,
                                             bias=cb[:, j:j + 1]))

            @blk.tensor
            def _(eng):
                eng.wait_ge(pr.sem("v_wc"), j + 1)
                if j == 0:
                    eng.wait_ge(pr.sem("lnC_dv"), NF)
                eng.wait_ge(pr.sem("v_cv"), j)
                inst = None
                for c in range(3):
                    c0, c1 = c * 512, min((c + 1) * 512, TOKH)
                    for kc in range(16):
                        inst = eng.matmul(ps_m[:, c0:c1], w_b[:, kc * 128:(kc + 1) * 128],
                                          xln[:, kc * TOKH + c0: kc * TOKH + c1],
                                          start=(kc == 0), stop=(kc == 15))
                pr.inc("pe_ip", inst)

    NXF = 35
    for xfc in range(NXF):
        fw = 128 if xfc < 34 else 64
        with nc.Block() as blk:
            @blk.sync
            def _(eng):
                for piece in range(2):
                    if xfc + piece > 0:
                        eng.wait_ge(pr.sem("v_wx"), 2 * xfc + piece)
                    pr.inc("d_wx", eng.dma_start(
                        wstg[:, 0:16 * fw].rearrange("p (kc f) -> p kc f", kc=16),
                        wxT[piece * 2048:(piece + 1) * 2048,
                            xfc * 128:xfc * 128 + fw].rearrange("(kc kp) f -> kp kc f", kp=128)), 16)

            @blk.vector
            def _(eng):
                for piece in range(2):
                    eng.wait_ge(pr.sem("d_wx"), 16 * (2 * xfc + piece + 1))
                    if xfc > 0 and piece == 0:
                        eng.wait_ge(pr.sem("pe_xp"), xfc)
                    pr.inc("v_wx", eng.tensor_copy(
                        w_b[:, piece * 2048:piece * 2048 + 16 * fw], wstg[:, 0:16 * fw]))
                eng.wait_ge(pr.sem("pe_xp"), xfc + 1)
                if xfc >= 2:
                    eng.wait_ge(pr.sem("d_xo%d" % (xfc % 2)), 16 * ((xfc - 2) // 2 + 1))
                pr.inc("v_xe", eng.tensor_copy(ostg[0:fw, (xfc % 2) * TOK:(xfc % 2 + 1) * TOK],
                                               ps_x[0:fw, :]))

            @blk.tensor
            def _(eng):
                eng.wait_ge(pr.sem("v_wx"), 2 * (xfc + 1))
                if xfc == 0:
                    eng.wait_ge(pr.sem("s_u"), 32)
                eng.wait_ge(pr.sem("v_xe"), xfc)
                inst = None
                for tc in range(2):
                    ts = slice(tc * 512, (tc + 1) * 512)
                    for kc in range(32):
                        inst = eng.matmul(ps_x[0:fw, ts],
                                          w_b[:, (kc // 16) * 2048 + (kc % 16) * fw:
                                              (kc // 16) * 2048 + (kc % 16) * fw + fw],
                                          u[:, kc * TOK + tc * 512: kc * TOK + (tc + 1) * 512],
                                          start=(kc == 0), stop=(kc == 31))
                pr.inc("pe_xp", inst)

            @blk.gpsimd
            def _(eng):
                eng.wait_ge(pr.sem("v_xe"), xfc + 1)
                src = ostg[0:fw, (xfc % 2) * TOK:(xfc % 2) * TOK + TOK]
                if xfc < 32:
                    dst = xfm[xfc * 128:(xfc + 1) * 128, :]
                elif xfc < 34:
                    dst = bcdt[(xfc - 32) * 128:(xfc - 31) * 128, :]
                else:
                    dst = bcdt[256:320, :]
                pr.inc("d_xo%d" % (xfc % 2), eng.dma_start(dst, src), 16)

    for j in range(32):
        with nc.Block() as blk:
            @blk.sync
            def _(eng):
                eng.wait_ge(pr.sem("v_wc"), 32 + j)
                if j == 0:
                    eng.wait_ge(pr.sem("v_wx"), 70)
                pr.inc("d_wz", eng.dma_start(
                    wstg[:].rearrange("p (kc f) -> p kc f", kc=16),
                    winT[:, 4096 + j * 128:4096 + (j + 1) * 128].rearrange("(kc kp) f -> kp kc f", kp=128)), 16)

            @blk.vector
            def _(eng):
                eng.wait_ge(pr.sem("d_wz"), 16 * (j + 1))
                if j > 0:
                    eng.wait_ge(pr.sem("pe_z"), j)
                pr.inc("v_wc", eng.tensor_copy(w_b[:, 0:2048], wstg[:]))

            @blk.scalar
            def _(eng):
                eng.wait_ge(pr.sem("pe_z"), j + 1)
                if j == 0:
                    eng.wait_ge(pr.sem("d_xo0"), pr.val("d_xo0"))
                    eng.wait_ge(pr.sem("d_xo1"), pr.val("d_xo1"))
                if j >= 2:
                    eng.wait_ge(pr.sem("d_zo%d" % (j % 2)), 16 * ((j - 2) // 2 + 1))
                pr.inc("s_z", eng.activation(ostg[:, (j % 2) * TOK:(j % 2 + 1) * TOK],
                                             ps_x[:], AF.Silu))

            @blk.tensor
            def _(eng):
                eng.wait_ge(pr.sem("v_wc"), 33 + j)
                eng.wait_ge(pr.sem("s_z"), j)
                if j == 0:
                    eng.wait_ge(pr.sem("v_xe"), NXF)
                inst = None
                for tc in range(2):
                    ts = slice(tc * 512, (tc + 1) * 512)
                    for kc in range(16):
                        inst = eng.matmul(ps_x[:, ts], w_b[:, kc * 128:(kc + 1) * 128],
                                          xln[:, kc * TOKH + 3 + tc * 512: kc * TOKH + 3 + (tc + 1) * 512],
                                          start=(kc == 0), stop=(kc == 15))
                pr.inc("pe_z", inst)

            @blk.gpsimd
            def _(eng):
                eng.wait_ge(pr.sem("s_z"), j + 1)
                pr.inc("d_zo%d" % (j % 2), eng.dma_start(zsf[j * 128:(j + 1) * 128, :],
                                             ostg[:, (j % 2) * TOK:(j % 2 + 1) * TOK]), 16)

    with nc.Block() as blk:
        @blk.sync
        def _(eng):
            pr.wait(eng, "d_xo0")
            pr.wait(eng, "d_xo1")
            pr.wait(eng, "d_zo0")
            pr.wait(eng, "d_zo1")
    return nc


def build_l0b():
    pr = Pr()
    nc = pr.nc
    xf = nc.dram_tensor("xf", [D, TOK], F32, kind="ExternalInput").ap()
    xfm = nc.dram_tensor("xfm", [4096, 1088], BF16, kind="ExternalInput").ap()
    bcdt = nc.dram_tensor("bcdt", [320, 1088], BF16, kind="ExternalInput").ap()
    zsf = nc.dram_tensor("zsf", [4096, TOK], BF16, kind="ExternalInput").ap()
    dtwT = nc.dram_tensor("dtwT", [64, 64], F32, kind="ExternalInput").ap()
    dtb = nc.dram_tensor("dtb", [64, 1], F32, kind="ExternalInput").ap()
    alog = nc.dram_tensor("alog", [64, 1], F32, kind="ExternalInput").ap()
    ddf = nc.dram_tensor("ddf", [128, 32], F32, kind="ExternalInput").ap()
    woutT = nc.dram_tensor("woutT", [4096, D], F32, kind="ExternalInput").ap()
    sinf = nc.dram_tensor("sinf", [128, 4096], F32, kind="ExternalInput").ap()
    onesf = nc.dram_tensor("onesf", [128, 128], F32, kind="ExternalInput").ap()
    repaf = nc.dram_tensor("repaf", [64, 4096], BF16, kind="ExternalInput").ap()
    trilf = nc.dram_tensor("trilf", [64, 4096], BF16, kind="ExternalInput").ap()
    repif = nc.dram_tensor("repif", [64, 4096], BF16, kind="ExternalInput").ap()
    repdf = nc.dram_tensor("repdf", [64, 4096], BF16, kind="ExternalInput").ap()
    triusf = nc.dram_tensor("triusf", [64, 64], BF16, kind="ExternalInput").ap()
    maskls = nc.dram_tensor("maskls", [64, 64], BF16, kind="ExternalInput").ap()
    cstf = nc.dram_tensor("cstf", [128, 8], F32, kind="ExternalInput").ap()
    xo = nc.dram_tensor("xo", [D, TOK], F32, kind="ExternalOutput").ap()
    sout = nc.dram_tensor("sout", [128, 4096], F32, kind="ExternalOutput").ap()

    Y = nc.alloc_sbuf_tensor("Y", [128, 32 * 512], BF16)
    S_run = nc.alloc_sbuf_tensor("S_run", [128, 4096], F32)
    Lam = nc.alloc_sbuf_tensor("Lam", [128, 1024], F32)
    dt = nc.alloc_sbuf_tensor("dt", [64, 1024], F32)
    scr1 = nc.alloc_sbuf_tensor("scr1", [128, 1024], F32)
    scr2 = nc.alloc_sbuf_tensor("scr2", [128, 1024], F32)
    vbuf = scr1[0:64, :]
    t1 = scr2[0:64, :]
    cs = nc.alloc_sbuf_tensor("cs", [64, 1024], F32)
    zeros64 = nc.alloc_sbuf_tensor("zeros64", [64, 64], F32)
    A_bf = nc.alloc_sbuf_tensor("A_bf", [64, 1024], BF16)
    dt_bf = nc.alloc_sbuf_tensor("dt_bf", [64, 1024], BF16)
    w_bf = nc.alloc_sbuf_tensor("w_bf", [64, 1024], BF16)
    decay_bf = nc.alloc_sbuf_tensor("decay_bf", [64, 1024], BF16)
    dtraw = nc.alloc_sbuf_tensor("dtraw", [64, 1024], BF16)
    lam = nc.alloc_sbuf_tensor("lam", [64, 16], F32)
    B_fm = nc.alloc_sbuf_tensor("B_fm", [128, 1024], BF16)
    C_fm = nc.alloc_sbuf_tensor("C_fm", [128, 1024], BF16)
    C_f32 = nc.alloc_sbuf_tensor("C_f32", [128, 1024], F32)
    Bz = nc.alloc_sbuf_tensor("Bz", [128, 128], BF16)
    Gbuf = nc.alloc_sbuf_tensor("Gbuf", [128, 4096], BF16)
    Gtmp = nc.alloc_sbuf_tensor("Gtmp", [128, 4096], BF16)
    rhsA = nc.alloc_sbuf_tensor("rhsA", [128, 4096], BF16)
    Xz = nc.alloc_sbuf_tensor("Xz", [128, 4096], BF16)
    Xlo = nc.alloc_sbuf_tensor("Xlo", [64, 4096], BF16)
    Bzl = nc.alloc_sbuf_tensor("Bzl", [64, 128], BF16)
    Xdt = nc.alloc_sbuf_tensor("Xdt", [128, 4096], BF16)
    Xw = nc.alloc_sbuf_tensor("Xw", [128, 4096], BF16)
    S_m = nc.alloc_sbuf_tensor("S_m", [128, 64], BF16)
    dorep = nc.alloc_sbuf_tensor("dorep", [128, 2048], BF16)
    ycat = nc.alloc_sbuf_tensor("ycat", [128, 1024], BF16)
    tmpv = nc.alloc_sbuf_tensor("tmpv", [128, 1024], F32)
    lamr = tmpv[0:1, 0:1024]
    repa = nc.alloc_sbuf_tensor("repa", [64, 4096], BF16)
    tril = nc.alloc_sbuf_tensor("tril", [64, 4096], BF16)
    repi = nc.alloc_sbuf_tensor("repi", [64, 4096], BF16)
    repd = nc.alloc_sbuf_tensor("repd", [64, 4096], BF16)
    trius = nc.alloc_sbuf_tensor("trius", [64, 64], BF16)
    maskl = nc.alloc_sbuf_tensor("maskl", [64, 64], BF16)
    ddc = nc.alloc_sbuf_tensor("ddc", [128, 32], F32)
    ones = nc.alloc_sbuf_tensor("ones", [128, 128], F32)
    cst = nc.alloc_sbuf_tensor("cst", [128, 8], F32)
    dtw_b = nc.alloc_sbuf_tensor("dtw_b", [64, 64], BF16)
    dtb_c = nc.alloc_sbuf_tensor("dtb_c", [64, 1], F32)
    alog_c = nc.alloc_sbuf_tensor("alog_c", [64, 1], F32)
    Acol = nc.alloc_sbuf_tensor("Acol", [64, 1], F32)
    zstg = nc.alloc_sbuf_tensor("zstg", [128, 2 * 512], BF16)
    xflstg = nc.alloc_sbuf_tensor("xflstg", [128, 2 * 512], BF16)
    wstg = nc.alloc_sbuf_tensor("wstg", [128, 2048], F32)
    wout_b = nc.alloc_sbuf_tensor("wout_b", [128, 4096], BF16)
    xstg = scr1
    ostg = scr2

    pool = nc.alloc_psum_tensor("pool", [128, 4096], F32)
    psYd = pool[:, 0:1024]
    psYo = pool[:, 1024:2048]
    psSt = pool[:, 2048:4096]
    psDo = pool[:, 0:2048]
    psOP = pool[:, 0:512]

    L64 = pool[0:64, :]

    with nc.Block() as blk:
        @blk.sync
        def _(eng):
            for name, dst, srcp in [("c", ones, onesf), ("c", cst, cstf), ("c", repa, repaf),
                                    ("c", tril, trilf), ("c", repi, repif), ("c", repd, repdf),
                                    ("c", trius, triusf), ("c", maskl, maskls), ("c", ddc, ddf),
                                    ("c", dtb_c, dtb), ("c", alog_c, alog)]:
                pr.inc("d_c", eng.dma_start(dst[:], srcp), 16)
            pr.inc("d_c", eng.dma_start(wstg[0:64, 0:64], dtwT), 16)
            pr.inc("d_c", eng.dma_start(dtraw[:], bcdt[0:64, 0:1024]), 16)
            pr.inc("d_c", eng.dma_start(B_fm[:], bcdt[64:192, 0:1024]), 16)
            pr.inc("d_c", eng.dma_start(C_fm[:], bcdt[192:320, 0:1024]), 16)
            pr.inc("d_c", eng.dma_start(S_run[:], sinf), 16)

    with nc.Block() as blk:
        @blk.vector
        def _(eng):
            pr.wait(eng, "d_c")
            pr.chain(eng, "vq", eng.memset(zeros64[:], 0.0))
            pr.chain(eng, "vq", eng.tensor_copy(C_f32[:], C_fm[:]))
            pr.inc("v_pp", eng.tensor_copy(dtw_b[:], wstg[0:64, 0:64]))

        @blk.tensor
        def _(eng):
            pr.wait(eng, "v_pp")
            inst = None
            for tc in range(2):
                inst = eng.matmul(pool[0:64, tc * 512:(tc + 1) * 512], dtw_b[:],
                                  dtraw[:, tc * 512:(tc + 1) * 512], start=True, stop=True)
            pr.inc("pe_pp", inst)

    with nc.Block() as blk:
        @blk.vector
        def _(eng):
            pr.wait(eng, "pe_pp")
            pr.inc("v_pp", eng.tensor_scalar(vbuf, pool[0:64, 0:1024], dtb_c[:, 0:1],
                                             None, ALU.add))

        @blk.scalar
        def _(eng):
            eng.wait_ge(pr.sem("v_pp"), 2)
            pr.chain(eng, "sq2", eng.activation(t1, vbuf, AF.Abs))
            pr.chain(eng, "sq2", eng.activation(t1, t1, AF.Exp, scale=-1.0))
            pr.inc("s_pp", eng.activation(t1, t1, AF.Ln, bias=1.0))
            pr.inc("s_pp", eng.activation(Acol[:], alog_c[:], AF.Exp))

    with nc.Block() as blk:
        @blk.vector
        def _(eng):
            eng.wait_ge(pr.sem("s_pp"), 2)
            pr.chain(eng, "vq", eng.scalar_tensor_tensor(dt[:], vbuf, 0.0, t1,
                                                         ALU.max, ALU.add))
            pr.chain(eng, "vq", eng.tensor_scalar(Acol[:], Acol[:], -1.0, None, ALU.mult))
            pr.chain(eng, "vq", eng.tensor_scalar(vbuf, dt[:], Acol[:, 0:1], None, ALU.mult))
            pr.chain(eng, "vq", eng.tensor_copy(A_bf[:], vbuf))
            pr.chain(eng, "vq", eng.tensor_copy(dt_bf[:], dt[:]))
            for c in range(16):
                sl = slice(c * 64, (c + 1) * 64)
                pr.chain(eng, "vq", eng.tensor_tensor_scan(cs[:, sl], vbuf[:, sl],
                                                           zeros64[:], 0.0, ALU.add, ALU.add))
            for c in range(16):
                sl = slice(c * 64, (c + 1) * 64)
                pr.chain(eng, "vq", eng.tensor_scalar(t1[:, sl], cs[:, sl],
                                                      cs[:, c * 64 + 63:c * 64 + 64], -1.0,
                                                      ALU.subtract, ALU.mult))
            for c in range(16):
                pr.inc("v_lc", eng.tensor_copy(lam[:, c:c + 1], cs[:, c * 64 + 63:c * 64 + 64]))
            pr.inc("v_pp", eng.tensor_copy(A_bf[:, 0:1], A_bf[:, 0:1]))

        @blk.scalar
        def _(eng):
            eng.wait_ge(pr.sem("v_pp"), 3)
            pr.chain(eng, "sq2", eng.activation(t1, t1, AF.Exp))
            pr.inc("s_pp", eng.activation(decay_bf[:], cs[:], AF.Exp))

    with nc.Block() as blk:
        @blk.scalar
        def _(eng):
            eng.wait_ge(pr.sem("v_lc"), 16)
            pr.inc("s_pp", eng.activation(lam[:], lam[:], AF.Exp))

        @blk.vector
        def _(eng):
            eng.wait_ge(pr.sem("s_pp"), 4)
            pr.inc("v_pp", eng.tensor_mul(w_bf[:], t1, dt[:]))

    with nc.Block() as blk:
        @blk.gpsimd
        def _(eng):
            eng.wait_ge(pr.sem("s_pp"), 4)
            pr.inc("d_lam", eng.dma_start(lamr[:], lam[:]), 16)

        @blk.tensor
        def _(eng):
            pr.wait(eng, "d_lam")
            inst = None
            for tc in range(2):
                inst = eng.matmul(pool[:, tc * 512:(tc + 1) * 512], ones[0:1, :],
                                  lamr[0:1, tc * 512:(tc + 1) * 512], start=True, stop=True)
            pr.inc("pe_lam", inst)

    with nc.Block() as blk:
        @blk.vector
        def _(eng):
            pr.wait(eng, "pe_lam")
            pr.inc("v_lam", eng.tensor_copy(Lam[:], pool[:, 0:1024]))

    for z in range(16):
        th, zl = z // 8, z % 8
        ev = (z % 2 == 0)
        Xc = Xz if ev else Xlo
        Bc = Bz if ev else Bzl
        with nc.Block() as blk:
            @blk.sync
            def _(eng):
                if ev:
                    eng.wait_ge(pr.sem("v_xw"), z)
                    pr.inc("d_xz", eng.dma_start(Xz[:], xfm[:, z * 64:z * 64 + 128],
                                                 transpose=True), 16)
                    eng.wait_ge(pr.sem("pe_st"), 2 * z)
                    pr.inc("d_bz", eng.dma_start(Bz[:], bcdt[64:192, z * 64:z * 64 + 128],
                                                 transpose=True), 16)

            @blk.gpsimd
            def _(eng):
                if not ev:
                    eng.wait_ge(pr.sem("d_xz"), 16 * (z // 2 + 1))
                    eng.wait_ge(pr.sem("v_xw"), z)
                    pr.inc("d_xlo", eng.dma_start(Xlo[:], Xz[64:128, :]), 16)
                    eng.wait_ge(pr.sem("d_bz"), 16 * (z // 2 + 1))
                    eng.wait_ge(pr.sem("pe_st"), 2 * z)
                    pr.inc("d_blo", eng.dma_start(Bzl[:], Bz[64:128, :]), 16)

            @blk.tensor
            def _(eng):
                if z == 0:
                    eng.wait_ge(pr.sem("v_pp"), 4)
                    pr.wait(eng, "v_lam")
                    pr.wait(eng, "d_c")
                if z == 8:
                    eng.wait_ge(pr.sem("v_oe"), 16)
                eng.wait_ge(pr.sem("v_ye"), 2 * z)
                eng.wait_ge(pr.sem("v_xw"), z)
                inst = None
                for J in range(32):
                    inst = eng.matmul(psDo[:, J * 64:(J + 1) * 64],
                                      repd[:, J * 128:(J + 1) * 128],
                                      decay_bf[:, z * 64:(z + 1) * 64], start=True, stop=True)
                pr.inc("pe_do", inst)
                eng.wait_ge(pr.sem("v_do"), z + 1)
                eng.wait_ge(pr.sem("v_sc"), 64 * z)
                inst = None
                for mc in range(8):
                    ms = slice(mc * 512, (mc + 1) * 512)
                    inst = eng.matmul(L64[:, ms], A_bf[:, z * 64:(z + 1) * 64],
                                      repa[:, ms], start=True, stop=True)
                pr.inc("pe_a", inst)
                eng.wait_ge(pr.sem("v_ra"), z + 1)
                inst = None
                for mc in range(8):
                    ms = slice(mc * 512, (mc + 1) * 512)
                    inst = eng.matmul(L64[:, ms], dt_bf[:, z * 64:(z + 1) * 64],
                                      repa[:, ms], start=True, stop=True)
                pr.inc("pe_b", inst)
                eng.wait_ge(pr.sem("v_xd"), z + 1)
                inst = None
                for mc in range(8):
                    ms = slice(mc * 512, (mc + 1) * 512)
                    inst = eng.matmul(L64[:, ms], trius[:], rhsA[0:64, ms],
                                      start=True, stop=True)
                pr.inc("pe_ds", inst)
                eng.wait_ge(pr.sem("s_ge"), z + 1)
                pr.inc("pe_s1", eng.matmul(L64[:, 0:64], C_fm[:, z * 64:(z + 1) * 64],
                                           B_fm[:, z * 64:(z + 1) * 64], start=True, stop=True))
                eng.wait_ge(pr.sem("v_sm"), z + 1)
                inst = None
                for mc in range(8):
                    ms = slice(mc * 512, (mc + 1) * 512)
                    inst = eng.matmul(L64[:, ms], S_m[0:64, :], repi[:, ms],
                                      start=True, stop=True)
                pr.inc("pe_rs", inst)
                eng.wait_ge(pr.sem("v_gf"), z + 1)
                inst = None
                for mc in range(8):
                    ms = slice(mc * 512, (mc + 1) * 512)
                    inst = eng.matmul(L64[:, ms], w_bf[:, z * 64:(z + 1) * 64],
                                      repa[:, ms], start=True, stop=True)
                pr.inc("pe_w", inst)
                for hf in range(2):
                    eng.wait_ge(pr.sem("v_xw"), z + 1)
                    eng.wait_ge(pr.sem("v_do"), z + 1)
                    eng.wait_ge(pr.sem("v_ye"), 2 * z + hf)
                    eng.wait_ge(pr.sem("v_sc"), 64 * z)
                    if zl == 0 and hf == 0 and th == 1:
                        eng.wait_ge(pr.sem("pe_op"), 16)
                        eng.wait_ge(pr.sem("v_yg"), 32)
                    inst = None
                    for j in range(16):
                        J = hf * 16 + j
                        inst = eng.matmul(psYd[0:64, j * 64:(j + 1) * 64],
                                          Xdt[0:64, (2 * J) * 64:(2 * J + 1) * 64],
                                          Gtmp[0:64, (2 * J) * 64:(2 * J + 1) * 64],
                                          start=True, stop=True)
                        inst = eng.matmul(psYd[64:128, j * 64:(j + 1) * 64],
                                          Xdt[0:64, (2 * J + 1) * 64:(2 * J + 2) * 64],
                                          Gtmp[0:64, (2 * J + 1) * 64:(2 * J + 2) * 64],
                                          start=True, stop=True, tile_position=(0, 64))
                    pr.inc("pe_yd", inst)
                    inst = None
                    for j in range(16):
                        J = hf * 16 + j
                        inst = eng.matmul(psYo[:, j * 64:(j + 1) * 64],
                                          S_run[:, J * 128:(J + 1) * 128],
                                          C_f32[:, z * 64:(z + 1) * 64], start=True, stop=True)
                    pr.inc("pe_yo", inst)
                    if ev:
                        eng.wait_ge(pr.sem("d_bz"), 16 * (z // 2 + 1))
                    else:
                        eng.wait_ge(pr.sem("d_blo"), 16 * (z // 2 + 1))
                    eng.wait_ge(pr.sem("v_sc"), 64 * z + 32 * hf)
                    inst = None
                    for hl in range(32):
                        h = hf * 32 + hl
                        inst = eng.matmul(psSt[:, hl * 64:(hl + 1) * 64], Bc[0:64, :],
                                          Xw[0:64, h * 64:(h + 1) * 64],
                                          start=True, stop=True)
                    pr.inc("pe_st", inst)

            @blk.scalar
            def _(eng):
                eng.wait_ge(pr.sem("pe_ds"), z + 1)
                eng.wait_ge(pr.sem("v_gf"), max(0, z - 1))
                pr.inc("s_ge", eng.activation(Gbuf[0:64, :], L64[:, 0:4096], AF.Exp))

            @blk.vector
            def _(eng):
                eng.wait_ge(pr.sem("pe_do"), z + 1)
                pr.inc("v_do", eng.tensor_copy(dorep[:], psDo[:]))
                eng.wait_ge(pr.sem("pe_a"), z + 1)
                eng.wait_ge(pr.sem("pe_ds"), max(0, z - 1) if z > 0 else 0)
                pr.inc("v_ra", eng.tensor_mul(rhsA[0:64, :], L64[:, 0:4096], tril[:]))
                eng.wait_ge(pr.sem("pe_b"), z + 1)
                if ev:
                    eng.wait_ge(pr.sem("d_xz"), 16 * (z // 2 + 1))
                else:
                    eng.wait_ge(pr.sem("d_xlo"), 16 * (z // 2 + 1))
                eng.wait_ge(pr.sem("pe_yd"), max(0, 2 * (z - 1)))
                pr.inc("v_xd", eng.tensor_mul(Xdt[0:64, :], Xc[0:64, :], L64[:, 0:4096]))
                eng.wait_ge(pr.sem("pe_s1"), z + 1)
                eng.wait_ge(pr.sem("pe_rs"), max(0, z - 1))
                pr.inc("v_sm", eng.tensor_mul(S_m[0:64, :], L64[:, 0:64], maskl[:]))
                eng.wait_ge(pr.sem("pe_rs"), z + 1)
                eng.wait_ge(pr.sem("pe_yd"), max(0, 2 * (z - 1)))
                pr.inc("v_gf", eng.tensor_mul(Gtmp[0:64, :], Gbuf[0:64, :], L64[:, 0:4096]))
                eng.wait_ge(pr.sem("pe_w"), z + 1)
                eng.wait_ge(pr.sem("pe_st"), max(0, 2 * (z - 1)))
                pr.inc("v_xw", eng.tensor_mul(Xw[0:64, :], Xc[0:64, :], L64[:, 0:4096]))
                for hf in range(2):
                    eng.wait_ge(pr.sem("pe_yd"), 2 * z + hf + 1)
                    eng.wait_ge(pr.sem("pe_yo"), 2 * z + hf + 1)
                    pr.chain(eng, "vq", eng.tensor_mul(tmpv[:], psYo[:],
                                                       dorep[:, hf * 1024:(hf + 1) * 1024]))
                    pr.chain(eng, "vq", eng.tensor_add(ycat[:], tmpv[:], psYd[:]))
                    inst = None
                    for j in range(16):
                        J = hf * 16 + j
                        inst = eng.tensor_copy(Y[:, J * 512 + zl * 64:J * 512 + (zl + 1) * 64],
                                               ycat[:, j * 64:(j + 1) * 64])
                    pr.inc("v_ye", inst)
                    eng.wait_ge(pr.sem("pe_st"), 2 * z + hf + 1)
                    for hl in range(32):
                        h = hf * 32 + hl
                        pr.inc("v_sc", eng.scalar_tensor_tensor(
                            S_run[:, h * 64:(h + 1) * 64], S_run[:, h * 64:(h + 1) * 64],
                            Lam[:, h * 16 + z:h * 16 + z + 1],
                            psSt[:, hl * 64:(hl + 1) * 64], ALU.mult, ALU.add))

        if zl == 7:
            for J in range(32):
                with nc.Block() as blk:
                    @blk.gpsimd
                    def _(eng):
                        if th > 0 or J >= 2:
                            eng.wait_ge(pr.sem("v_yg"),
                                        32 * th + J - 1 if J >= 2 else 32 * th)
                        pr.inc("d_zs%d" % (J % 2), eng.dma_start(
                            zstg[:, (J % 2) * 512:(J % 2 + 1) * 512],
                            zsf[J * 128:(J + 1) * 128, th * 512:(th + 1) * 512]), 16)
                        pr.inc("d_xl%d" % (J % 2), eng.dma_start(
                            xflstg[:, (J % 2) * 512:(J % 2 + 1) * 512],
                            xfm[J * 128:(J + 1) * 128, th * 512:(th + 1) * 512]), 16)

                    @blk.vector
                    def _(eng):
                        eng.wait_ge(pr.sem("d_zs%d" % (J % 2)), 16 * (th * 16 + J // 2 + 1))
                        eng.wait_ge(pr.sem("d_xl%d" % (J % 2)), 16 * (th * 16 + J // 2 + 1))
                        eng.wait_ge(pr.sem("v_ye"), 16 * (th + 1))
                        pr.chain(eng, "vq", eng.scalar_tensor_tensor(
                            Y[:, J * 512:(J + 1) * 512],
                            xflstg[:, (J % 2) * 512:(J % 2 + 1) * 512],
                            ddc[:, J:J + 1], Y[:, J * 512:(J + 1) * 512],
                            ALU.mult, ALU.add))
                        pr.inc("v_yg", eng.tensor_mul(Y[:, J * 512:(J + 1) * 512],
                                                      Y[:, J * 512:(J + 1) * 512],
                                                      zstg[:, (J % 2) * 512:(J % 2 + 1) * 512]))
            for od in range(16):
                odg = th * 16 + od
                with nc.Block() as blk:
                    @blk.sync
                    def _(eng):
                        for piece in range(2):
                            if odg + piece > 0:
                                eng.wait_ge(pr.sem("v_wo"), 2 * odg + piece)
                            pr.inc("d_wo", eng.dma_start(
                                wstg[:].rearrange("p (kc f) -> p kc f", kc=16),
                                woutT[piece * 2048:(piece + 1) * 2048,
                                      od * 128:(od + 1) * 128].rearrange(
                                          "(kc kp) f -> kp kc f", kp=128)), 16)

                    @blk.gpsimd
                    def _(eng):
                        eng.wait_ge(pr.sem("v_oe"), odg)
                        pr.inc("d_xr%d" % (od % 2), eng.dma_start(
                            xstg[:, (od % 2) * 512:(od % 2 + 1) * 512],
                            xf[od * 128:(od + 1) * 128, th * 512:(th + 1) * 512]), 16)
                        if odg > 0:
                            po = (odg - 1) % 16
                            pt = (odg - 1) // 16
                            pr.inc("d_or%d" % (po % 2), eng.dma_start(
                                xo[po * 128:(po + 1) * 128, pt * 512:(pt + 1) * 512],
                                ostg[:, (po % 2) * 512:(po % 2 + 1) * 512]), 16)

                    @blk.vector
                    def _(eng):
                        for piece in range(2):
                            eng.wait_ge(pr.sem("d_wo"), 16 * (2 * odg + piece + 1))
                            if piece == 0 and odg > 0:
                                eng.wait_ge(pr.sem("pe_op"), odg)
                            pr.inc("v_wo", eng.tensor_copy(
                                wout_b[:, piece * 2048:(piece + 1) * 2048], wstg[:]))
                        eng.wait_ge(pr.sem("pe_op"), odg + 1)
                        eng.wait_ge(pr.sem("d_xr%d" % (od % 2)), 16 * (th * 8 + od // 2 + 1))
                        if odg >= 2:
                            eng.wait_ge(pr.sem("d_or%d" % (od % 2)), 16 * (th * 8 + od // 2))
                        pr.inc("v_oe", eng.tensor_add(
                            ostg[:, (od % 2) * 512:(od % 2 + 1) * 512],
                            xstg[:, (od % 2) * 512:(od % 2 + 1) * 512], psOP[:]))

                    @blk.tensor
                    def _(eng):
                        eng.wait_ge(pr.sem("v_wo"), 2 * odg + 2)
                        if od == 0:
                            eng.wait_ge(pr.sem("v_yg"), 32 * (th + 1))
                            eng.wait_ge(pr.sem("v_sc"), 64 * 8 * (th + 1))
                        eng.wait_ge(pr.sem("v_oe"), odg)
                        inst = None
                        for fc in range(32):
                            inst = eng.matmul(psOP[:],
                                              wout_b[:, fc * 128:(fc + 1) * 128],
                                              Y[:, fc * 512:(fc + 1) * 512],
                                              start=(fc == 0), stop=(fc == 31))
                        pr.inc("pe_op", inst)

    with nc.Block() as blk:
        @blk.gpsimd
        def _(eng):
            pr.wait(eng, "v_oe")
            pr.inc("d_or1", eng.dma_start(xo[15 * 128:16 * 128, 512:1024],
                                          ostg[:, 512:1024]), 16)
            pr.wait(eng, "v_sc")
            pr.inc("d_so", eng.dma_start(sout, S_run[:]), 16)

        @blk.sync
        def _(eng):
            pr.wait(eng, "d_or0")
            pr.wait(eng, "d_or1")
            pr.wait(eng, "d_so")
    return nc


def _shard_x(x):
    outs = []
    for c in range(NCORES):
        seq, half = c // 2, c % 2
        t0 = half * TOK
        blk = np.ascontiguousarray(x[seq, t0:t0 + TOK, :].T)
        if half == 0:
            halo = np.zeros((D, 3), np.float32)
        else:
            halo = np.ascontiguousarray(x[seq, t0 - 3:t0, :].T)
        outs.append((blk, halo))
    return outs


_ONES = np.ones((128, 128), np.float32)
_CST = np.zeros((128, 8), np.float32)
_CST[:, 0] = LN_EPS


def _ffn_maps(inputs, xf_percore):
    wgT = np.ascontiguousarray(np.asarray(inputs["L0_ffn_gate_w"], np.int32).T)
    wuT = np.ascontiguousarray(np.asarray(inputs["L0_ffn_up_w"], np.int32).T)
    wdT = np.ascontiguousarray(np.asarray(inputs["L0_ffn_down_w"], np.int32).T)
    gsc = np.ascontiguousarray(np.asarray(inputs["L0_ffn_gate_scale"], np.float32).reshape(64, 128).T)
    usc = np.ascontiguousarray(np.asarray(inputs["L0_ffn_up_scale"], np.float32).reshape(64, 128).T)
    dsc = np.ascontiguousarray(np.asarray(inputs["L0_ffn_down_scale"], np.float32).reshape(16, 128).T)
    return [dict(xf=xf_percore[c], wgT=wgT, wuT=wuT, wdT=wdT,
                 gsc=gsc, usc=usc, dsc=dsc, onesf=_ONES, cstf=_CST) for c in range(NCORES)]


_ONESB = np.ones((128, 128), BF)
_IDENT = np.eye(128, dtype=np.float32)


def _attn_maps(inputs, xf_percore):
    kv_k = np.asarray(inputs["L1_attn_kv_k"], np.float32).reshape(4096, 16, 128)
    kv_v = np.asarray(inputs["L1_attn_kv_v"], np.float32).reshape(4096, 16, 128)
    kTf = np.ascontiguousarray(kv_k.transpose(1, 2, 0))
    vTf = np.ascontiguousarray(kv_v.transpose(1, 0, 2))
    wqT = np.ascontiguousarray(np.asarray(inputs["L1_attn_Wq"], np.float32).T)
    woT = np.ascontiguousarray(np.asarray(inputs["L1_attn_Wo"], np.float32).T)
    return [dict(xf=xf_percore[c], kTf=kTf, vTf=vTf, wqT=wqT, woT=woT,
                 onesf=_ONES, onesb=_ONESB, cstf=_CST) for c in range(NCORES)]


def kernel(**inputs):
    x = np.asarray(inputs["L0_attn_x"], np.float32)
    shards = _shard_x(x)
    xf_percore = [s[0] for s in shards]

    nc_a = build_l0a()
    res_a = run_bass_kernel_spmd(nc_a, _l0a_maps(inputs, x), core_ids=list(range(NCORES)))
    a_outs = {c: {k: np.asarray(res_a.results[c][k]) for k in ("xfm", "bcdt", "zsf")}
              for c in range(NCORES)}

    nc_b = build_l0b()
    zero_s = np.zeros((128, 4096), np.float32)
    sin1 = {c: zero_s for c in range(NCORES)}
    maps_b1 = _l0b_maps(inputs, xf_percore, a_outs, sin1)
    res_b1 = run_bass_kernel_spmd(nc_b, [maps_b1[c] for c in range(NCORES)],
                                  core_ids=list(range(NCORES)))
    sin2 = {}
    for c in range(NCORES):
        if c % 2 == 1:
            sin2[c] = np.asarray(res_b1.results[c - 1]["sout"], np.float32)
        else:
            sin2[c] = zero_s
    maps_b2 = _l0b_maps(inputs, xf_percore, a_outs, sin2)
    res_b2 = run_bass_kernel_spmd(nc_b, [maps_b2[c] for c in range(NCORES)],
                                  core_ids=list(range(NCORES)))
    x1 = [np.asarray(res_b2.results[c]["xo"], np.float32) for c in range(NCORES)]

    nc_f = build_l0ffn()
    res_f = run_bass_kernel_spmd(nc_f, _ffn_maps(inputs, x1), core_ids=list(range(NCORES)))
    x2 = [np.asarray(res_f.results[c]["xo"], np.float32) for c in range(NCORES)]

    nc_t = build_l1attn()
    res_t = run_bass_kernel_spmd(nc_t, _attn_maps(inputs, x2), core_ids=list(range(NCORES)))
    x3 = [np.asarray(res_t.results[c]["xo"], np.float32) for c in range(NCORES)]

    nc_m = build_l1ffn()
    res_m = run_bass_kernel_spmd(nc_m, _moe_maps(inputs, x3), core_ids=list(range(NCORES)))
    outs = [np.asarray(res_m.results[c]["xo"], np.float32).T for c in range(NCORES)]

    y = np.stack([np.concatenate([outs[2 * s], outs[2 * s + 1]], axis=0) for s in range(4)])
    return y.astype(np.float32)
